# version 05
# baseline (speedup 1.0000x reference)
import numpy as np
import ml_dtypes
from contextlib import ExitStack

import concourse.bass as bass
import concourse.mybir as mybir
from concourse.bass_utils import run_bass_kernel_spmd

F32 = mybir.dt.float32
BF16 = mybir.dt.bfloat16
AF = mybir.ActivationFunctionType
ALU = mybir.AluOpType
BF16NP = ml_dtypes.bfloat16

NCORES = 8
NB = 4
T = 2048
TCH = 512
NCH = T // TCH
D = 1024
DB = D // 128
ZC = 2080
CB = 17
SPLIT = 32

MAGIC = float(np.float32(1.5 * 2 ** 23))
TWO_PI = float(2 * np.pi)
PI = float(np.pi)
EPS_LN = 1e-5
EPS_RMS = float(np.finfo(np.float32).eps)


class Prog:

    ENGS = ("t", "v", "s", "g", "y")

    def __init__(self, nc, block, ctx):
        self.nc = nc
        self.block = block
        self.ctx = ctx
        self.cnt = {}
        self.sem = {}
        for e in self.ENGS:
            self.sem[e] = ctx.enter_context(nc.semaphore(f"prog_{e}"))
            self.cnt[e] = 0
        self._body = {"t": block.tensor, "v": block.vector, "s": block.scalar,
                      "g": block.gpsimd, "y": block.sync}
        self.dma = {}
        self.consts = {}
        self.pend = {e: [] for e in self.ENGS}
        self.FLUSH_N = 384
        self.w = {}
        self.r = {}
        self.nsteps = 0

    def constf(self, val):
        val = float(val)
        if val not in self.consts:
            t = self.ctx.enter_context(self.nc.sbuf_tensor(f"cst{len(self.consts)}", [128, 1], F32))
            self.consts[val] = t
            self.emit("v", lambda E: E.memset(t[:, :], val), wr=[(t, "*")])
        return self.consts[val]

    def dma_sem(self, name):
        if name not in self.dma:
            self.dma[name] = [self.ctx.enter_context(self.nc.semaphore(f"dma_{name}")), 0]
        return self.dma[name]

    @staticmethod
    def _name(buf):
        return buf if isinstance(buf, str) else buf.name

    def _conflict_keys(self, d, key):
        if key == "*":
            return list(d.keys())
        out = [k for k in (key, "*") if k in d]
        return out

    def emit(self, eng, fn, rd=(), wr=(), extra=(), inc=True, dma=None, tag=None):
        toks = list(extra)
        if dma is not None:
            semrec = self.dma_sem(dma)
            if semrec[1] > 0:
                toks.append((dma, semrec[1]))
        for buf, key in rd:
            n = self._name(buf)
            for k in self._conflict_keys(self.w.get(n, {}), key):
                toks.append(self.w[n][k])
        for buf, key in wr:
            n = self._name(buf)
            for k in self._conflict_keys(self.w.get(n, {}), key):
                toks.append(self.w[n][k])
            for k in self._conflict_keys(self.r.get(n, {}), key):
                toks.extend(self.r[n][k])

        waits = []
        need_drain = False
        seen = {}
        for w in toks:
            if w is None:
                continue
            k, val = w
            if k == eng:
                need_drain = True
            else:
                seen[k] = max(seen.get(k, 0), val)
        for k, val in seen.items():
            if val > 0:
                h = self.sem[k] if k in self.ENGS else self.dma[k][0]
                waits.append((h, val))

        if dma is not None:
            semrec[1] += 16
            token = (dma, semrec[1])

            def body(E, waits=waits, need_drain=need_drain, fn=fn, sh=semrec[0]):
                for h, v in waits:
                    E.wait_ge(h, v)
                if need_drain:
                    E.drain()
                fn(E).then_inc(sh, 16)
        elif inc:
            self.cnt[eng] += 1
            token = (eng, self.cnt[eng])

            def body(E, waits=waits, need_drain=need_drain, fn=fn, sh=self.sem[eng]):
                for h, v in waits:
                    E.wait_ge(h, v)
                if need_drain:
                    E.drain()
                ins = fn(E)
                if ins is not None:
                    ins.then_inc(sh, 1)
        else:
            token = (eng, self.cnt[eng])

            def body(E, waits=waits, need_drain=need_drain, fn=fn):
                for h, v in waits:
                    E.wait_ge(h, v)
                if need_drain:
                    E.drain()
                fn(E)

        self.pend[eng].append(body)
        if len(self.pend[eng]) >= self.FLUSH_N:
            self.flush(eng)
        self.nsteps += 1
        for buf, key in rd:
            n = self._name(buf)
            self.r.setdefault(n, {}).setdefault(key, []).append(token)
        for buf, key in wr:
            n = self._name(buf)
            if key == "*":
                self.w[n] = {}
                self.r[n] = {}
            self.w.setdefault(n, {})[key] = token
            rr = self.r.setdefault(n, {})
            rr[key] = []
        return token

    def flush(self, e=None):
        engs = self.ENGS if e is None else (e,)
        for en in engs:
            fns = self.pend[en]
            if not fns:
                continue
            self.pend[en] = []

            def body(E, fns=fns):
                for f in fns:
                    f(E)
            self._body[en](body)

    def barrier(self):
        self.flush()
        snap = {e: self.cnt[e] for e in self.ENGS}
        dsnap = {k: v[1] for k, v in self.dma.items()}
        for e in self.ENGS:
            def body(E, e=e):
                for o in self.ENGS:
                    if o != e and snap[o] > 0:
                        E.wait_ge(self.sem[o], snap[o])
                for k, v in dsnap.items():
                    if v > 0:
                        E.wait_ge(self.dma[k][0], v)
                if e in ("t", "v", "s", "g"):
                    E.drain()
            self._body[e](body)
        self.w = {}
        self.r = {}


def bslice(t, blk, width, lo=0, hi=None):
    hi = width if hi is None else hi
    return t[:, blk * width + lo: blk * width + hi]


def build_phase_a(nc, P, ctx, dr, nb=NB):
    e = ctx.enter_context
    c_pi2 = P.constf(PI / 2)
    c_one = P.constf(1.0)
    c_eps = P.constf(EPS_LN)

    wi_bf = e(nc.sbuf_tensor("wi_bf", [128, DB * ZC], BF16))
    wo_bf = e(nc.sbuf_tensor("wo_bf", [128, DB * D], BF16))
    kcol = e(nc.sbuf_tensor("sb_kcol", [128, CB * 4], F32))
    th = e(nc.sbuf_tensor("th", [128, DB], F32))
    th2p = e(nc.sbuf_tensor("th2p", [128, DB], F32))
    sscol = e(nc.sbuf_tensor("sscol", [128, 1], F32))
    slopes = e(nc.sbuf_tensor("sb_slopes", [128, 1], F32))
    dmat = e(nc.sbuf_tensor("sb_dmat", [128, T], F32))
    logw = dmat
    e_b = e(nc.sbuf_tensor("e_b", [128, DB * 128], BF16))
    e_bf32 = e(nc.sbuf_tensor("e_bf32", [128, DB * 128], F32))
    e2 = e(nc.sbuf_tensor("e2", [128, DB * 32], BF16))
    wre = e(nc.sbuf_tensor("wre", [128, 32], F32))
    wim = e(nc.sbuf_tensor("wim", [128, 32], F32))
    nsc = e(nc.sbuf_tensor("nsc", [128, 1], F32))
    wre_sb = e(nc.sbuf_tensor("wre_sb", [128, 32], BF16))
    wim_sb = e(nc.sbuf_tensor("wim_sb", [128, 32], BF16))
    bd_re = e(nc.sbuf_tensor("bd_re", [128, 128], BF16))
    bd_im = e(nc.sbuf_tensor("bd_im", [128, 128], BF16))
    ones_r = e(nc.sbuf_tensor("ones_r", [1, 128], BF16))
    ones_rf = e(nc.sbuf_tensor("ones_rf", [1, 128], F32))
    ones_c = e(nc.sbuf_tensor("ones_c", [128, 1], BF16))
    zcol = e(nc.sbuf_tensor("zcol", [128, 1], F32))

    xt_f = e(nc.sbuf_tensor("xt_f", [128, DB * TCH], F32))
    xt_b = e(nc.sbuf_tensor("xt_b", [128, DB * TCH], BF16))
    zwin = [e(nc.sbuf_tensor(f"zwin{i}", [128, TCH + 3], F32)) for i in range(2)]
    tails = e(nc.sbuf_tensor("tails", [128, CB * 3], F32))
    zc_t = [e(nc.sbuf_tensor(f"zc{i}", [128, TCH], F32)) for i in range(2)]
    tmp = [e(nc.sbuf_tensor(f"atmp{i}", [128, TCH], F32)) for i in range(5)]
    pw_bf = e(nc.sbuf_tensor("pw_bf", [128, TCH], BF16))
    gate_bf = e(nc.sbuf_tensor("gate_bf", [128, DB * TCH], BF16))
    mer_re = e(nc.sbuf_tensor("mer_re", [128, DB * TCH], F32))
    mer_im = e(nc.sbuf_tensor("mer_im", [128, DB * TCH], F32))
    cs_den = e(nc.sbuf_tensor("cs_den", [128, TCH], F32))
    cs_re = e(nc.sbuf_tensor("cs_re", [128, DB * TCH], F32))
    cs_im = e(nc.sbuf_tensor("cs_im", [128, DB * TCH], F32))
    car_den = e(nc.sbuf_tensor("car_den", [128, 1], F32))
    car_re = e(nc.sbuf_tensor("car_re", [128, DB], F32))
    car_im = e(nc.sbuf_tensor("car_im", [128, DB], F32))
    invd = e(nc.sbuf_tensor("invd", [128, TCH], F32))
    sq_bf = e(nc.sbuf_tensor("sq_bf", [128, DB * TCH], BF16))
    sqa = e(nc.sbuf_tensor("sqa", [128, TCH], F32))
    sqb = e(nc.sbuf_tensor("sqb", [128, TCH], F32))
    al_bf = e(nc.sbuf_tensor("al_bf", [128, TCH], F32))
    ren_re = e(nc.sbuf_tensor("ren_re", [128, TCH], BF16))
    ren_im = e(nc.sbuf_tensor("ren_im", [128, TCH], BF16))
    gy_bf = sq_bf
    r_f = cs_im
    r_bf = xt_b
    rsq_bf = mer_im[:, 0:DB * TCH // 2].bitcast(BF16)
    x1_f = cs_re
    x1_b = gate_bf
    row = [tmp[i][0:1, 0:TCH] for i in range(4)]

    ps = [e(nc.psum_tensor(f"ps{i}", [128, TCH], F32)) for i in range(8)]

    P.emit("g", lambda E: E.dma_start(wi_bf[:, :].rearrange("p (k c) -> p k c", k=DB), dr["W_in"].rearrange("(k p) c -> p k c", p=128)), wr=[(wi_bf, "*")], dma="wA")
    P.emit("g", lambda E: E.dma_start(wo_bf[:, :].rearrange("p (k c) -> p k c", k=DB), dr["W_out"].rearrange("(k p) c -> p k c", p=128)), wr=[(wo_bf, "*")], dma="wA")
    for nm, buf, sl in [("kcol", kcol, None), ("theta_c", th, None), ("theta_2p", th2p, None),
                        ("score_scale", sscol, 32), ("slopes", slopes, 32), ("dmat", dmat, 32),
                        ("E_b", e_b, 32), ("E_2", e2, None), ("W_re", wre, 32), ("W_im", wim, 32),
                        ("norm_scale", nsc, 64)]:
        P.emit("y", lambda E, nm=nm, buf=buf, sl=sl: E.dma_start(
            buf[0:sl, :] if sl else buf[:, :], dr[nm][:, :]), wr=[(buf, "*")], dma="cA")
    P.emit("y", lambda E: E.dma_start(e_bf32[0:32, :], dr["E_bf"][:, :]), wr=[(e_bf32, "*")], dma="cA")

    P.emit("s", lambda E: E.activation(tmp[0][0:32, 0:1], slopes[0:32, :], AF.Abs), rd=[(slopes, 0)], wr=[(tmp[0], "*")])
    P.emit("s", lambda E: E.activation(tmp[1][0:32, 0:1], tmp[0][0:32, 0:1], AF.Exp, scale=-1.0), rd=[(tmp[0], "*")], wr=[(tmp[1], "*")])
    P.emit("s", lambda E: E.activation(tmp[2][0:32, 0:1], tmp[1][0:32, 0:1], AF.Ln, bias=c_one[0:32, 0:1]), rd=[(tmp[1], "*"), (c_one, "*")], wr=[(tmp[2], "*")])
    P.emit("s", lambda E: E.activation(tmp[3][0:32, 0:1], slopes[0:32, :], AF.Relu), rd=[(slopes, 0)], wr=[(tmp[3], "*")])
    P.emit("v", lambda E: E.tensor_add(tmp[4][0:32, 0:1], tmp[2][0:32, 0:1], tmp[3][0:32, 0:1]), rd=[(tmp[2], "*"), (tmp[3], "*")], wr=[(tmp[4], "*")])
    P.emit("v", lambda E: E.tensor_scalar_mul(sqa[0:32, 0:1], tmp[4][0:32, 0:1], -1.0), rd=[(tmp[4], "*")], wr=[(sqa, "*")])
    P.emit("v", lambda E: E.tensor_scalar_mul(logw[0:32, :], dmat[0:32, :], sqa[0:32, 0:1]), rd=[(sqa, "*"), (dmat, 0)], wr=[(logw, "*")])

    P.emit("v", lambda E: E.tensor_scalar_mul(wre_sb[0:32, :], wre[0:32, :], nsc[0:32, 0:1]), rd=[(wre, 0), (nsc, 0)], wr=[(wre_sb, "*")])
    P.emit("v", lambda E: E.tensor_scalar_mul(wim_sb[0:32, :], wim[0:32, :], nsc[32:64, 0:1]), rd=[(wim, 0), (nsc, 0)], wr=[(wim_sb, "*")])
    P.emit("v", lambda E: E.memset(bd_re[:, :], 0.0), wr=[(bd_re, "*")])
    P.emit("v", lambda E: E.memset(bd_im[:, :], 0.0), wr=[(bd_im, "*")])
    for j in range(4):
        P.emit("v", lambda E, j=j: E.tensor_copy(bd_re[32 * j:32 * j + 32, 32 * j:32 * j + 32], wre_sb[0:32, :]), rd=[(wre_sb, 0)], wr=[(bd_re, j)])
        P.emit("v", lambda E, j=j: E.tensor_copy(bd_im[32 * j:32 * j + 32, 32 * j:32 * j + 32], wim_sb[0:32, :]), rd=[(wim_sb, 0)], wr=[(bd_im, j)])
    P.emit("v", lambda E: E.memset(ones_r[:, :], 1.0), wr=[(ones_r, "*")])
    P.emit("v", lambda E: E.memset(ones_rf[:, :], 1.0), wr=[(ones_rf, "*")])
    P.emit("v", lambda E: E.memset(ones_c[:, :], 1.0), wr=[(ones_c, "*")])
    P.emit("v", lambda E: E.memset(zcol[:, :], 0.0), wr=[(zcol, "*")])
    P.barrier()

    for b in range(nb):
        P.emit("v", lambda E: E.memset(tails[:, :], 0.0), wr=[(tails, "*")])
        P.emit("v", lambda E: E.memset(car_den[:, :], 0.0), wr=[(car_den, "*")])
        P.emit("v", lambda E: E.memset(car_re[:, :], 0.0), wr=[(car_re, "*")])
        P.emit("v", lambda E: E.memset(car_im[:, :], 0.0), wr=[(car_im, "*")])

        for c in range(NCH):
            lo = c * TCH
            P.emit("y", lambda E, b=b, lo=lo: E.dma_start(
                xt_f[:, :].rearrange("p (k t) -> p k t", k=DB), dr["xT"][b, :, lo:lo + TCH].rearrange("(k p) t -> p k t", p=128)),
                wr=[(xt_f, "*")], dma="xf")
            P.emit("g", lambda E, b=b, lo=lo: E.dma_start(
                xt_b[:, :].rearrange("p (k t) -> p k t", k=DB), dr["xT"][b, :, lo:lo + TCH].rearrange("(k p) t -> p k t", p=128)),
                wr=[(xt_b, "*")], dma="xb")

            blk_order = [16] + list(range(0, 8)) + list(range(8, 16))
            for bi, j in enumerate(blk_order):
                pz, zw, zt = ps[bi % 2], zwin[bi % 2], zc_t[bi % 2]
                rows = 32 if j == 16 else 128

                def zmm(E, j=j, pz=pz, rows=rows):
                    for k in range(DB):
                        ii = E.matmul(pz[0:rows, :],
                                      wi_bf[:, k * ZC + j * 128: k * ZC + j * 128 + rows],
                                      bslice(xt_b, k, TCH),
                                      start=(k == 0), stop=(k == DB - 1))
                    return ii
                P.emit("t", zmm, rd=[(wi_bf, 0), (xt_b, 0)], wr=[(pz, "*")])
                P.emit("v", lambda E, zw=zw, pz=pz, rows=rows: E.tensor_copy(zw[0:rows, 3:3 + TCH], pz[0:rows, :]),
                       rd=[(pz, "*")], wr=[(zw, "zbody")])
                P.emit("v", lambda E, zw=zw, j=j, rows=rows: E.tensor_copy(zw[0:rows, 0:3], tails[0:rows, 3 * j:3 * j + 3]),
                       rd=[(tails, j)], wr=[(zw, "ztail")])
                P.emit("v", lambda E, zt=zt, zw=zw, j=j, rows=rows: E.tensor_scalar_mul(
                    zt[0:rows, :], zw[0:rows, 0:TCH], kcol[0:rows, 4 * j:4 * j + 1]),
                    rd=[(zw, "zbody"), (zw, "ztail"), (kcol, 0)], wr=[(zt, "*")])
                for tap in range(1, 4):
                    P.emit("v", lambda E, zt=zt, zw=zw, j=j, tap=tap, rows=rows: E.scalar_tensor_tensor(
                        zt[0:rows, :], zw[0:rows, tap:tap + TCH], kcol[0:rows, 4 * j + tap:4 * j + tap + 1],
                        zt[0:rows, :], ALU.mult, ALU.add),
                        rd=[(zw, "zbody"), (zw, "ztail"), (kcol, 0), (zt, "*")], wr=[(zt, "*")])
                P.emit("v", lambda E, zw=zw, j=j, rows=rows: E.tensor_copy(
                    tails[0:rows, 3 * j:3 * j + 3], zw[0:rows, TCH:TCH + 3]),
                    rd=[(zw, "zbody")], wr=[(tails, j)])

                if j == 16:
                    P.emit("v", lambda E, zt=zt: E.tensor_scalar_mul(tmp[0][0:32, :], zt[0:32, :], sscol[0:32, 0:1]),
                           rd=[(zt, "*"), (sscol, 0)], wr=[(tmp[0], "*")])
                    P.emit("v", lambda E, lo=lo: E.tensor_add(tmp[1][0:32, :], tmp[0][0:32, :], logw[0:32, lo:lo + TCH]),
                           rd=[(tmp[0], "*"), (logw, 0)], wr=[(tmp[1], "*")])
                    P.emit("s", lambda E: E.activation(pw_bf[0:32, :], tmp[1][0:32, :], AF.Exp),
                           rd=[(tmp[1], "*")], wr=[(pw_bf, "*")])
                elif j < 8:
                    P.emit("t", lambda E, j=j: E.matmul(ps[2][:, :], e_b[0:32, j * 128:(j + 1) * 128], pw_bf[0:32, :], start=True, stop=True),
                           rd=[(e_b, 0), (pw_bf, "*")], wr=[(ps[2], "*")])
                    P.emit("v", lambda E, zt=zt, j=j: E.tensor_scalar_mul(tmp[0][:, :], zt[:, :], th[:, j:j + 1]),
                           rd=[(zt, "*"), (th, 0)], wr=[(tmp[0], "*")])
                    P.emit("v", lambda E, zt=zt, j=j: E.tensor_scalar(tmp[1][:, :], zt[:, :], th2p[:, j:j + 1], MAGIC, ALU.mult, ALU.add),
                           rd=[(zt, "*"), (th2p, 0)], wr=[(tmp[1], "*")])
                    P.emit("v", lambda E: E.tensor_scalar(tmp[2][:, :], tmp[1][:, :], -MAGIC, -TWO_PI, ALU.add, ALU.mult),
                           rd=[(tmp[1], "*")], wr=[(tmp[2], "*")])
                    P.emit("v", lambda E: E.tensor_add(tmp[3][:, :], tmp[0][:, :], tmp[2][:, :]),
                           rd=[(tmp[0], "*"), (tmp[2], "*")], wr=[(tmp[3], "*")])
                    P.emit("s", lambda E: E.activation(tmp[4][:, :], tmp[3][:, :], AF.Sin),
                           rd=[(tmp[3], "*")], wr=[(tmp[4], "*")])
                    P.emit("v", lambda E, zt=zt, j=j: E.tensor_scalar(tmp[1][:, :], zt[:, :], th2p[:, j:j + 1], 0.25, ALU.mult, ALU.add),
                           rd=[(zt, "*"), (th2p, 0)], wr=[(tmp[1], "*")])
                    P.emit("v", lambda E: E.tensor_scalar(tmp[2][:, :], tmp[1][:, :], MAGIC, -MAGIC, ALU.add, ALU.add),
                           rd=[(tmp[1], "*")], wr=[(tmp[2], "*")])
                    P.emit("v", lambda E: E.tensor_scalar_mul(tmp[1][:, :], tmp[2][:, :], -TWO_PI),
                           rd=[(tmp[2], "*")], wr=[(tmp[1], "*")])
                    P.emit("v", lambda E: E.tensor_add(tmp[2][:, :], tmp[0][:, :], tmp[1][:, :]),
                           rd=[(tmp[0], "*"), (tmp[1], "*")], wr=[(tmp[2], "*")])
                    P.emit("s", lambda E: E.activation(tmp[3][:, :], tmp[2][:, :], AF.Sin, bias=c_pi2[:, 0:1]),
                           rd=[(tmp[2], "*"), (c_pi2, "*")], wr=[(tmp[3], "*")])
                    P.emit("v", lambda E, j=j: E.tensor_mul(bslice(mer_im, j, TCH), tmp[4][:, :], ps[2][:, :]),
                           rd=[(tmp[4], "*"), (ps[2], "*")], wr=[(mer_im, j)])
                    P.emit("v", lambda E, j=j: E.tensor_mul(bslice(mer_re, j, TCH), tmp[3][:, :], ps[2][:, :]),
                           rd=[(tmp[3], "*"), (ps[2], "*")], wr=[(mer_re, j)])
                else:
                    P.emit("s", lambda E, zt=zt, j=j: E.activation(bslice(gate_bf, j - 8, TCH), zt[:, :], AF.Silu),
                           rd=[(zt, "*")], wr=[(gate_bf, j - 8)])

            P.emit("v", lambda E: E.tensor_tensor_scan(
                cs_den[0:32, :], pw_bf[0:32, :], zcol[0:32, :].to_broadcast([32, TCH]),
                car_den[0:32, 0:1], ALU.add, ALU.add),
                rd=[(pw_bf, "*"), (zcol, "*"), (car_den, "*")], wr=[(cs_den, "*")])
            P.emit("v", lambda E: E.tensor_copy(car_den[0:32, 0:1], cs_den[0:32, TCH - 1:TCH]),
                   rd=[(cs_den, "*")], wr=[(car_den, "*")])
            for j in range(DB):
                P.emit("v", lambda E, j=j: E.tensor_tensor_scan(
                    bslice(cs_re, j, TCH), bslice(mer_re, j, TCH),
                    zcol[:, :].to_broadcast([128, TCH]), car_re[:, j:j + 1], ALU.add, ALU.add),
                    rd=[(mer_re, j), (zcol, "*"), (car_re, j)], wr=[(cs_re, j)])
                P.emit("v", lambda E, j=j: E.tensor_copy(car_re[:, j:j + 1], bslice(cs_re, j, TCH, TCH - 1, TCH)),
                       rd=[(cs_re, j)], wr=[(car_re, j)])
                P.emit("v", lambda E, j=j: E.tensor_tensor_scan(
                    bslice(cs_im, j, TCH), bslice(mer_im, j, TCH),
                    zcol[:, :].to_broadcast([128, TCH]), car_im[:, j:j + 1], ALU.add, ALU.add),
                    rd=[(mer_im, j), (zcol, "*"), (car_im, j)], wr=[(cs_im, j)])
                P.emit("v", lambda E, j=j: E.tensor_copy(car_im[:, j:j + 1], bslice(cs_im, j, TCH, TCH - 1, TCH)),
                       rd=[(cs_im, j)], wr=[(car_im, j)])

            P.emit("v", lambda E: E.tensor_scalar_max(invd[0:32, :], cs_den[0:32, :], 1e-4),
                   rd=[(cs_den, "*")], wr=[(invd, "a")])
            P.emit("v", lambda E: E.reciprocal(invd[0:32, :], invd[0:32, :]),
                   rd=[(invd, "a")], wr=[(invd, "a")])
            for j in range(DB):
                P.emit("s", lambda E, j=j: E.activation(sqa[:, :], bslice(cs_re, j, TCH), AF.Square),
                       rd=[(cs_re, j)], wr=[(sqa, "*")])
                P.emit("s", lambda E, j=j: E.activation(sqb[:, :], bslice(cs_im, j, TCH), AF.Square),
                       rd=[(cs_im, j)], wr=[(sqb, "*")])
                P.emit("v", lambda E, j=j: E.tensor_add(bslice(sq_bf, j, TCH), sqa[:, :], sqb[:, :]),
                       rd=[(sqa, "*"), (sqb, "*")], wr=[(sq_bf, j)])

            def qmm(E):
                for j in range(DB):
                    ii = E.matmul(ps[3][0:32, :], e2[:, j * 32:(j + 1) * 32], bslice(sq_bf, j, TCH),
                                  start=(j == 0), stop=(j == DB - 1))
                return ii
            P.emit("t", qmm, rd=[(e2, 0)] + [(sq_bf, j) for j in range(DB)], wr=[(ps[3], "*")])
            P.emit("v", lambda E: E.tensor_mul(sqa[0:32, :], invd[0:32, :], invd[0:32, :]),
                   rd=[(invd, "a")], wr=[(sqa, "*")])
            P.emit("v", lambda E: E.tensor_mul(sqb[0:32, :], ps[3][0:32, :], sqa[0:32, :]),
                   rd=[(ps[3], "*"), (sqa, "*")], wr=[(sqb, "*")])
            P.emit("s", lambda E: E.activation(sqa[0:32, :], sqb[0:32, :], AF.Ln, scale=1.0 / (2 * SPLIT), bias=c_eps[0:32, 0:1]),
                   rd=[(sqb, "*"), (c_eps, "*")], wr=[(sqa, "*")])
            P.emit("s", lambda E: E.activation(sqb[0:32, :], sqa[0:32, :], AF.Exp, scale=-0.5),
                   rd=[(sqa, "*")], wr=[(sqb, "*")])
            P.emit("v", lambda E: E.tensor_mul(al_bf[0:32, :], invd[0:32, :], sqb[0:32, :]),
                   rd=[(invd, "a"), (sqb, "*")], wr=[(al_bf, "*")])

            for j in range(DB):
                P.emit("t", lambda E, j=j: E.matmul(ps[2][:, :], e_bf32[0:32, j * 128:(j + 1) * 128], al_bf[0:32, :], start=True, stop=True),
                       rd=[(e_bf32, "*"), (al_bf, "*")], wr=[(ps[2], "*")])
                P.emit("v", lambda E, j=j: E.tensor_mul(ren_re[:, :], bslice(cs_re, j, TCH), ps[2][:, :]),
                       rd=[(cs_re, j), (ps[2], "*")], wr=[(ren_re, "*")])
                P.emit("v", lambda E, j=j: E.tensor_mul(ren_im[:, :], bslice(cs_im, j, TCH), ps[2][:, :]),
                       rd=[(cs_im, j), (ps[2], "*")], wr=[(ren_im, "*")])

                def ymm(E):
                    E.matmul(ps[3][:, :], bd_re[:, :], ren_re[:, :], start=True, stop=False)
                    return E.matmul(ps[3][:, :], bd_im[:, :], ren_im[:, :], start=False, stop=True)
                P.emit("t", ymm, rd=[(bd_re, "*"), (bd_im, "*"), (ren_re, "*"), (ren_im, "*")], wr=[(ps[3], "*")])
                P.emit("v", lambda E, j=j: E.tensor_mul(bslice(gy_bf, j, TCH), ps[3][:, :], bslice(gate_bf, j, TCH)),
                       rd=[(ps[3], "*"), (gate_bf, j)], wr=[(gy_bf, j)])

            for m in range(DB):
                pz = ps[4 + (m % 2)]

                def wmm(E, m=m, pz=pz):
                    for k in range(DB):
                        ii = E.matmul(pz[:, :], wo_bf[:, k * D + m * 128: k * D + m * 128 + 128],
                                      bslice(gy_bf, k, TCH), start=(k == 0), stop=(k == DB - 1))
                    return ii
                P.emit("t", wmm, rd=[(wo_bf, 0)] + [(gy_bf, k) for k in range(DB)], wr=[(pz, "*")])
                P.emit("v", lambda E, m=m, pz=pz: E.tensor_add(bslice(r_f, m, TCH), pz[:, :], bslice(xt_f, m, TCH)),
                       rd=[(pz, "*"), (xt_f, 0)], wr=[(r_f, m)])

            for m in range(DB):
                P.emit("s", lambda E, m=m: E.activation(bslice(r_bf, m, TCH), bslice(r_f, m, TCH), AF.Copy),
                       rd=[(r_f, m)], wr=[(r_bf, m)])
                P.emit("s", lambda E, m=m: E.activation(bslice(rsq_bf, m, TCH), bslice(r_f, m, TCH), AF.Square),
                       rd=[(r_f, m)], wr=[(rsq_bf, m)])

            def smm(E):
                for k in range(DB):
                    ii = E.matmul(ps[6][0:1, :], ones_c[:, :], bslice(r_bf, k, TCH), start=(k == 0), stop=(k == DB - 1))
                return ii
            P.emit("t", smm, rd=[(ones_c, "*")] + [(r_bf, k) for k in range(DB)], wr=[(ps[6], "*")])

            def s2mm(E):
                for k in range(DB):
                    ii = E.matmul(ps[7][0:1, :], ones_c[:, :], bslice(rsq_bf, k, TCH), start=(k == 0), stop=(k == DB - 1))
                return ii
            P.emit("t", s2mm, rd=[(ones_c, "*")] + [(rsq_bf, k) for k in range(DB)], wr=[(ps[7], "*")])
            P.emit("v", lambda E: E.tensor_scalar_mul(row[0][:, :], ps[6][0:1, :], 1.0 / D), rd=[(ps[6], "*")], wr=[(row[0], "*")])
            P.emit("v", lambda E: E.tensor_scalar_mul(row[1][:, :], ps[7][0:1, :], 1.0 / D), rd=[(ps[7], "*")], wr=[(row[1], "*")])
            P.emit("v", lambda E: E.tensor_mul(row[2][:, :], row[0][:, :], row[0][:, :]), rd=[(row[0], "*")], wr=[(row[2], "*")])
            P.emit("v", lambda E: E.tensor_sub(row[3][:, :], row[1][:, :], row[2][:, :]), rd=[(row[1], "*"), (row[2], "*")], wr=[(row[3], "*")])
            P.emit("s", lambda E: E.activation(row[1][0:1, :], row[3][0:1, :], AF.Ln, bias=c_eps[0:1, 0:1]), rd=[(row[3], "*"), (c_eps, "*")], wr=[(row[1], "*")])
            P.emit("s", lambda E: E.activation(row[2][:, :], row[1][:, :], AF.Exp, scale=-0.5), rd=[(row[1], "*")], wr=[(row[2], "*")])
            P.emit("v", lambda E: E.tensor_mul(row[3][:, :], row[0][:, :], row[2][:, :]), rd=[(row[0], "*"), (row[2], "*")], wr=[(row[3], "*")])
            P.emit("t", lambda E: E.matmul(ps[6][:, :], ones_rf[:, :], row[2][:, :], start=True, stop=True),
                   rd=[(ones_rf, "*"), (row[2], "*")], wr=[(ps[6], "*")])
            P.emit("t", lambda E: E.matmul(ps[7][:, :], ones_rf[:, :], row[3][:, :], start=True, stop=True),
                   rd=[(ones_rf, "*"), (row[3], "*")], wr=[(ps[7], "*")])
            for m in range(DB):
                P.emit("v", lambda E, m=m: E.tensor_mul(bslice(x1_f, m, TCH), bslice(r_f, m, TCH), ps[6][:, :]),
                       rd=[(r_f, m), (ps[6], "*")], wr=[(x1_f, m)])
                P.emit("v", lambda E, m=m: E.tensor_sub(bslice(x1_f, m, TCH), bslice(x1_f, m, TCH), ps[7][:, :]),
                       rd=[(x1_f, m), (ps[7], "*")], wr=[(x1_f, m)])
                P.emit("v", lambda E, m=m: E.tensor_copy(bslice(x1_b, m, TCH), bslice(x1_f, m, TCH)),
                       rd=[(x1_f, m)], wr=[(x1_b, m)])

            P.emit("y", lambda E, b=b, lo=lo: E.dma_start(
                dr["x1T"][b, :, lo:lo + TCH].rearrange("(k p) t -> p k t", p=128), x1_f[:, :].rearrange("p (k t) -> p k t", k=DB)),
                rd=[(x1_f, m) for m in range(DB)], dma="x1o")
            P.emit("g", lambda E, b=b, lo=lo: E.dma_start(
                dr["x1bT"][b, :, lo:lo + TCH].rearrange("(k p) t -> p k t", p=128), x1_b[:, :].rearrange("p (k t) -> p k t", k=DB)),
                rd=[(x1_b, m) for m in range(DB)], dma="x1bo")
            P.barrier()


def _host_prep_a(d):
    h = {}
    h["W_in"] = np.ascontiguousarray(d["L0_attn_W_in"])
    h["W_out"] = np.ascontiguousarray(d["L0_attn_W_out"])
    ck = np.asarray(d["L0_attn_conv_kernel"]).reshape(4, ZC)
    kcol = np.zeros((128, CB * 4), np.float32)
    for j in range(CB):
        rows = min(128, ZC - j * 128)
        kcol[:rows, 4 * j:4 * j + 4] = ck[:, j * 128:j * 128 + rows].T
    h["kcol"] = kcol
    thf = np.asarray(d["L0_attn_theta"]).reshape(D).astype(np.float32)
    h["theta_c"] = np.ascontiguousarray(thf.reshape(DB, 128).T)
    h["theta_2p"] = np.ascontiguousarray((thf / (2 * np.pi)).astype(np.float32).reshape(DB, 128).T)
    h["score_scale"] = np.asarray(d["L0_attn_score_scale"]).reshape(32, 1).astype(np.float32)
    h["slopes"] = np.concatenate([np.asarray(d["L0_attn_decay_slopes"]),
                                  np.asarray(d["L0_attn_anchor_slopes"])]).reshape(32, 1).astype(np.float32)
    pos = np.arange(T, dtype=np.float32)
    dmat = np.zeros((32, T), np.float32)
    dmat[0:28, :] = (np.float32(T - 1) - pos)[None, :]
    dmat[28:32, :] = pos[None, :]
    h["dmat"] = dmat
    eb = np.zeros((32, DB * 128), np.float32)
    for j in range(DB):
        for p in range(128):
            eb[4 * j + p // 32, j * 128 + p] = 1.0
    h["E_b"] = eb.astype(BF16NP)
    h["E_bf"] = eb.astype(np.float32)
    e2m = np.zeros((128, DB * 32), np.float32)
    for j in range(DB):
        for p in range(128):
            e2m[p, j * 32 + 4 * j + p // 32] = 1.0
    h["E_2"] = e2m.astype(BF16NP)
    h["W_re"] = np.ascontiguousarray(d["L0_attn_W_re"])
    h["W_im"] = np.ascontiguousarray(d["L0_attn_W_im"])
    h["norm_scale"] = np.asarray(d["L0_attn_norm_scale"]).reshape(64, 1).astype(np.float32)
    return h


A_INPUTS = [("xT", [NB, D, T], F32), ("W_in", [D, ZC], F32), ("W_out", [D, D], F32),
            ("kcol", [128, CB * 4], F32), ("theta_c", [128, DB], F32), ("theta_2p", [128, DB], F32),
            ("score_scale", [32, 1], F32), ("slopes", [32, 1], F32), ("dmat", [32, T], F32),
            ("E_b", [32, DB * 128], BF16), ("E_bf", [32, DB * 128], F32), ("E_2", [128, DB * 32], BF16),
            ("W_re", [32, 32], F32), ("W_im", [32, 32], F32), ("norm_scale", [64, 1], F32)]


def _build_a(nb=NB):
    nc = bass.Bass("TRN2", target_bir_lowering=False, debug=False)
    dr = {}
    for nm, shp, dt in A_INPUTS:
        shp = [nb, D, T] if nm == "xT" else shp
        dr[nm] = nc.dram_tensor(nm, shp, dt, kind="ExternalInput").ap()
    dr["x1T"] = nc.dram_tensor("x1T", [nb, D, T], F32, kind="ExternalOutput").ap()
    dr["x1bT"] = nc.dram_tensor("x1bT", [nb, D, T], BF16, kind="ExternalOutput").ap()
    with ExitStack() as ctx:
        block = ctx.enter_context(nc.Block())
        prog = Prog(nc, block, ctx)
        build_phase_a(nc, prog, ctx, dr, nb=nb)
    return nc


def run_phase_a(inputs, n_cores=NCORES, nb=NB):
    d = {k: np.asarray(v) for k, v in inputs.items()}
    h = _host_prep_a(d)
    xT_all = np.ascontiguousarray(np.asarray(d["L0_attn_x"]).transpose(0, 2, 1))
    nc = _build_a(nb=nb)
    in_maps = []
    for cid in range(n_cores):
        m = dict(h)
        m["xT"] = xT_all[cid * nb:(cid + 1) * nb]
        in_maps.append(m)
    res = run_bass_kernel_spmd(nc, in_maps, core_ids=list(range(n_cores)))
    x1 = np.concatenate([np.asarray(r["x1T"]) for r in res.results], axis=0)
    x1b = np.concatenate([np.asarray(r["x1bT"]) for r in res.results], axis=0)
    return x1, x1b




def ln_stats(P, rbf_ap, rsqbf_ap, row, psS, psQ, ones_c, ones_rf, c_eps, rd_keys):

    def smm(E):
        for k in range(DB):
            ii = E.matmul(psS[0:1, :], ones_c[:, :], rbf_ap(k), start=(k == 0), stop=(k == DB - 1))
        return ii
    P.emit("t", smm, rd=rd_keys[0] + [(ones_c, "*")], wr=[(psS, "*")])

    def qmm(E):
        for k in range(DB):
            ii = E.matmul(psQ[0:1, :], ones_c[:, :], rsqbf_ap(k), start=(k == 0), stop=(k == DB - 1))
        return ii
    P.emit("t", qmm, rd=rd_keys[1] + [(ones_c, "*")], wr=[(psQ, "*")])
    P.emit("v", lambda E: E.tensor_scalar_mul(row[0][:, :], psS[0:1, :], 1.0 / D), rd=[(psS, "*")], wr=[(row[0], "*")])
    P.emit("v", lambda E: E.tensor_scalar_mul(row[1][:, :], psQ[0:1, :], 1.0 / D), rd=[(psQ, "*")], wr=[(row[1], "*")])
    P.emit("v", lambda E: E.tensor_mul(row[2][:, :], row[0][:, :], row[0][:, :]), rd=[(row[0], "*")], wr=[(row[2], "*")])
    P.emit("v", lambda E: E.tensor_sub(row[3][:, :], row[1][:, :], row[2][:, :]), rd=[(row[1], "*"), (row[2], "*")], wr=[(row[3], "*")])
    P.emit("s", lambda E: E.activation(row[1][0:1, :], row[3][0:1, :], AF.Ln, bias=c_eps[0:1, 0:1]), rd=[(row[3], "*"), (c_eps, "*")], wr=[(row[1], "*")])
    P.emit("s", lambda E: E.activation(row[2][0:1, :], row[1][0:1, :], AF.Exp, scale=-0.5), rd=[(row[1], "*")], wr=[(row[2], "*")])
    P.emit("v", lambda E: E.tensor_mul(row[3][:, :], row[0][:, :], row[2][:, :]), rd=[(row[0], "*"), (row[2], "*")], wr=[(row[3], "*")])
    P.emit("t", lambda E: E.matmul(psS[:, :], ones_rf[:, :], row[2][:, :], start=True, stop=True),
           rd=[(ones_rf, "*"), (row[2], "*")], wr=[(psS, "*")])
    P.emit("t", lambda E: E.matmul(psQ[:, :], ones_rf[:, :], row[3][:, :], start=True, stop=True),
           rd=[(ones_rf, "*"), (row[3], "*")], wr=[(psQ, "*")])




def emit_comb(P, lg_psum, out_ap, out_key, bufs):
    lg_sb, m8, scr, mrk, mskA, mskB, wcol = (bufs[k] for k in
        ("lg_sb", "m8", "scr", "mrk", "mskA", "mskB", "wcol"))
    big, nbig = 3e38, -3e38
    P.emit("v", lambda E: E.tensor_copy(lg_sb[:, :], lg_psum), rd=[(lg_psum.tensor, "*")], wr=[(lg_sb, "*")])
    P.emit("v", lambda E: E.max(out=m8[:, :], in_=lg_sb[:, :]), rd=[(lg_sb, "*")], wr=[(m8, "*")])
    P.emit("v", lambda E: E.memset(scr[:, :], big), wr=[(scr, "*")])
    P.emit("v", lambda E: E.tensor_copy(scr[:, 0:1], m8[:, 0:1]), rd=[(m8, "*")], wr=[(scr, "*")])
    P.emit("v", lambda E: E.match_replace(out=mrk[:, :], in_to_replace=scr[:, :], in_values=lg_sb[:, :], imm_value=nbig),
           rd=[(scr, "*"), (lg_sb, "*")], wr=[(mrk, "*")])
    P.emit("v", lambda E: E.tensor_scalar(mskA[:, :], mrk[:, :], -1e37, None, ALU.is_le),
           rd=[(mrk, "*")], wr=[(mskA, "*")])
    P.emit("v", lambda E: E.tensor_copy(scr[:, 0:2], m8[:, 0:2]), rd=[(m8, "*")], wr=[(scr, "*")])
    P.emit("v", lambda E: E.match_replace(out=mrk[:, :], in_to_replace=scr[:, :], in_values=lg_sb[:, :], imm_value=nbig),
           rd=[(scr, "*"), (lg_sb, "*")], wr=[(mrk, "*")])
    P.emit("v", lambda E: E.tensor_scalar(mskB[:, :], mrk[:, :], -1e37, None, ALU.is_le),
           rd=[(mrk, "*")], wr=[(mskB, "*")])
    P.emit("v", lambda E: E.tensor_sub(wcol[0][:, :], m8[:, 0:1], m8[:, 1:2]), rd=[(m8, "*")], wr=[(wcol[0], "*")])
    P.emit("s", lambda E: E.activation(wcol[1][:, :], wcol[0][:, :], AF.Exp, scale=-1.0), rd=[(wcol[0], "*")], wr=[(wcol[1], "*")])
    P.emit("v", lambda E: E.tensor_scalar_add(wcol[1][:, :], wcol[1][:, :], 1.0), rd=[(wcol[1], "*")], wr=[(wcol[1], "*")])
    P.emit("v", lambda E: E.reciprocal(wcol[1][:, :], wcol[1][:, :]), rd=[(wcol[1], "*")], wr=[(wcol[1], "*")])
    P.emit("v", lambda E: E.tensor_scalar(wcol[2][:, :], wcol[1][:, :], -1.0, 1.0, ALU.mult, ALU.add),
           rd=[(wcol[1], "*")], wr=[(wcol[2], "*")])
    P.emit("v", lambda E: E.tensor_sub(wcol[0][:, :], wcol[1][:, :], wcol[2][:, :]),
           rd=[(wcol[1], "*"), (wcol[2], "*")], wr=[(wcol[0], "*")])
    P.emit("v", lambda E: E.tensor_scalar_mul(mskB[:, :], mskB[:, :], wcol[2][:, :]),
           rd=[(mskB, "*"), (wcol[2], "*")], wr=[(mskB, "*")])
    P.emit("v", lambda E: E.scalar_tensor_tensor(out_ap, mskA[:, :], wcol[0][:, :], mskB[:, :], ALU.mult, ALU.add),
           rd=[(mskA, "*"), (wcol[0], "*"), (mskB, "*")], wr=[out_key])

NE = 8
NI = 4096
IB = NI // 128
HGRP = 2
DGRP = 8
HALF = T // 2


def build_phase_b(nc, P, ctx, dr, nb=NB):
    e = ctx.enter_context
    c_eps = P.constf(EPS_LN)
    big = 3e38
    nbig = -3e38

    aw_bf = e(nc.sbuf_tensor("aw_bf", [128, DB * D], BF16))
    gw_bf = e(nc.sbuf_tensor("gw_bf", [128, DB * NE], BF16))
    g1c = e(nc.sbuf_tensor("g1c", [128, DB], F32))
    b1c = e(nc.sbuf_tensor("b1c", [128, DB], F32))
    g2c = e(nc.sbuf_tensor("g2c", [128, DB], F32))
    b2c = e(nc.sbuf_tensor("b2c", [128, DB], F32))
    ident = e(nc.sbuf_tensor("identB", [128, 128], F32))
    ones_c = e(nc.sbuf_tensor("ones_cB", [128, 1], BF16))
    ones_rf = e(nc.sbuf_tensor("ones_rfB", [1, 128], F32))

    x1b_ch = e(nc.sbuf_tensor("x1b_ch", [128, DB * TCH], BF16))
    xf_t = [e(nc.sbuf_tensor(f"xf_t{i}", [128, TCH], F32)) for i in range(2)]
    u_t0 = e(nc.sbuf_tensor("u_t0", [128, TCH], F32))
    u_t = [u_t0, u_t0]
    t1b = e(nc.sbuf_tensor("t1b", [128, DB * TCH], BF16))
    sqb_ch = t1b
    t1lo = a_buf_view = None
    h_f = e(nc.sbuf_tensor("h_f", [128, DB * TCH], F32))
    hb_ch = x1b_ch
    t_half = e(nc.sbuf_tensor("t_half", [128, DB * HALF], BF16))
    a_buf = e(nc.sbuf_tensor("a_buf", [128, IB * HALF], BF16))
    t1lo = a_buf[:, 0:DB * TCH]
    gwlo_bf = e(nc.sbuf_tensor("gwlo_bf", [128, DB * NE], BF16))
    gw_stg = e(nc.sbuf_tensor("gw_stg", [128, NE], F32))
    acc = e(nc.sbuf_tensor("acc", [128, DB * HALF], F32))
    comb_all = e(nc.sbuf_tensor("comb_all", [128, (T // 128) * NE], F32))
    ctstage = e(nc.sbuf_tensor("ctstage", [1, NE * 128], F32))
    combT_sb = e(nc.sbuf_tensor("combT_sb", [1, TCH], F32))
    crep = e(nc.sbuf_tensor("crep", [128, 2 * TCH], BF16))
    lg_sb = e(nc.sbuf_tensor("lg_sb", [128, 8], F32))
    m8 = e(nc.sbuf_tensor("m8B", [128, 8], F32))
    scr = e(nc.sbuf_tensor("scrB", [128, 8], F32))
    mrk = e(nc.sbuf_tensor("mrkB", [128, 8], F32))
    mskA = e(nc.sbuf_tensor("mskA", [128, 8], F32))
    mskB = e(nc.sbuf_tensor("mskB", [128, 8], F32))
    wcol = [e(nc.sbuf_tensor(f"wcol{i}", [128, 1], F32)) for i in range(3)]
    sg_bf = e(nc.sbuf_tensor("sg_bf", [128, TCH], BF16))
    uu_bf = e(nc.sbuf_tensor("uu_bf", [128, TCH], BF16))
    wg_buf = [e(nc.sbuf_tensor(f"wg{i}", [128, DB * HGRP * 128], BF16)) for i in range(2)]
    wu_buf = [e(nc.sbuf_tensor(f"wu{i}", [128, DB * HGRP * 128], BF16)) for i in range(2)]
    wd_buf = [e(nc.sbuf_tensor(f"wd{i}", [128, DGRP * 256], BF16)) for i in range(2)]
    row = [ctstage[0:1, 0:TCH], ctstage[0:1, TCH:2 * TCH]] + \
          [e(nc.sbuf_tensor(f"brow{i}", [1, TCH], F32)) for i in range(2)]

    ps = [e(nc.psum_tensor(f"psB{i}", [128, TCH], F32)) for i in range(8)]

    P.emit("g", lambda E: E.dma_start(aw_bf[:, :].rearrange("p (k c) -> p k c", k=DB),
                                      dr["awT"].rearrange("(k p) c -> p k c", p=128)), wr=[(aw_bf, "*")], dma="wB")
    P.emit("g", lambda E: E.dma_start(gw_bf[:, :].rearrange("p (k c) -> p k c", k=DB),
                                      dr["gwT"].rearrange("(k p) c -> p k c", p=128)), wr=[(gw_bf, "*")], dma="wB")
    for nm, buf in [("ln1_g", g1c), ("ln1_b", b1c), ("ln2_g", g2c), ("ln2_b", b2c)]:
        P.emit("y", lambda E, nm=nm, buf=buf: E.dma_start(
            buf[:, :], dr[nm][:, :]), wr=[(buf, "*")], dma="cB")
    P.emit("y", lambda E: E.dma_start(ident[:, :], dr["identity"][:, :]), wr=[(ident, "*")], dma="cB")
    P.emit("v", lambda E: E.memset(ones_c[:, :], 1.0), wr=[(ones_c, "*")])
    P.emit("v", lambda E: E.memset(ones_rf[:, :], 1.0), wr=[(ones_rf, "*")])
    for k in range(DB):
        P.emit("y", lambda E, k=k: E.dma_start(gw_stg[:, :], dr["gwT"][k * 128:(k + 1) * 128, :]),
               wr=[(gw_stg, "*")], dma="cB")
        P.emit("v", lambda E, k=k: E.tensor_sub(gwlo_bf[:, k * NE:(k + 1) * NE], gw_stg[:, :], gw_bf[:, k * NE:(k + 1) * NE]),
               rd=[(gw_stg, "*"), (gw_bf, "*")], wr=[(gwlo_bf, "*")])
    P.barrier()

    for b in range(nb):
        for c in range(NCH):
            lo = c * TCH
            P.emit("y", lambda E, b=b, lo=lo: E.dma_start(
                x1b_ch[:, :].rearrange("p (k t) -> p k t", k=DB),
                dr["x1bT"][b, :, lo:lo + TCH].rearrange("(k p) t -> p k t", p=128)),
                wr=[(x1b_ch, "*")], dma="xb")
            for k in range(DB):
                P.emit("s", lambda E, k=k: E.activation(bslice(sqb_ch, k, TCH), bslice(x1b_ch, k, TCH), AF.Square),
                       rd=[(x1b_ch, "*")], wr=[(sqb_ch, k)])
            ln_stats(P, lambda k: bslice(x1b_ch, k, TCH), lambda k: bslice(sqb_ch, k, TCH),
                     row, ps[6], ps[7], ones_c, ones_rf, c_eps,
                     [[(x1b_ch, "*")], [(sqb_ch, k) for k in range(DB)]])
            for m in range(DB):
                xf = xf_t[m % 2]
                ut = u_t[m % 2]
                P.emit("y", lambda E, b=b, lo=lo, m=m, xf=xf: E.dma_start(
                    xf[:, :], dr["x1T"][b, m * 128:(m + 1) * 128, lo:lo + TCH]),
                    wr=[(xf, "*")], dma="xf")
                P.emit("v", lambda E, m=m, xf=xf, ut=ut: E.tensor_mul(ut[:, :], xf[:, :], ps[6][:, :]),
                       rd=[(xf, "*"), (ps[6], "*")], wr=[(ut, "*")])
                P.emit("v", lambda E, m=m, ut=ut: E.tensor_sub(ut[:, :], ut[:, :], ps[7][:, :]),
                       rd=[(ut, "*"), (ps[7], "*")], wr=[(ut, "*")])
                P.emit("v", lambda E, m=m, ut=ut: E.scalar_tensor_tensor(
                    bslice(t1b, m, TCH), ut[:, :], g1c[:, m:m + 1], b1c[:, m:m + 1].to_broadcast([128, TCH]),
                    ALU.mult, ALU.add), rd=[(ut, "*"), (g1c, "*"), (b1c, "*")], wr=[(t1b, m)])
            for m in range(DB):
                pz = ps[m % 2]
                xf = xf_t[m % 2]

                def hmm(E, m=m, pz=pz):
                    for k in range(DB):
                        ii = E.matmul(pz[:, :], aw_bf[:, k * D + m * 128: k * D + m * 128 + 128],
                                      bslice(t1b, k, TCH), start=(k == 0), stop=(k == DB - 1))
                    return ii
                P.emit("t", hmm, rd=[(aw_bf, "*")] + [(t1b, k) for k in range(DB)], wr=[(pz, "*")])
                P.emit("y", lambda E, b=b, lo=lo, m=m, xf=xf: E.dma_start(
                    xf[:, :], dr["x1T"][b, m * 128:(m + 1) * 128, lo:lo + TCH]),
                    wr=[(xf, "*")], dma="xf")
                P.emit("v", lambda E, m=m, pz=pz, xf=xf: E.tensor_add(bslice(h_f, m, TCH), pz[:, :], xf[:, :]),
                       rd=[(pz, "*"), (xf, "*")], wr=[(h_f, m)])
            P.emit("y", lambda E, b=b, lo=lo: E.dma_start(
                dr["h_scr"][b, :, lo:lo + TCH].rearrange("(k p) t -> p k t", p=128),
                h_f[:, :].rearrange("p (k t) -> p k t", k=DB)),
                rd=[(h_f, m) for m in range(DB)], wr=[("h_scr", lo)], dma="ho")
            for k in range(DB):
                P.emit("s", lambda E, k=k: E.activation(bslice(hb_ch, k, TCH), bslice(h_f, k, TCH), AF.Copy),
                       rd=[(h_f, k)], wr=[(hb_ch, k)])
                P.emit("s", lambda E, k=k: E.activation(bslice(sqb_ch, k, TCH), bslice(h_f, k, TCH), AF.Square),
                       rd=[(h_f, k)], wr=[(sqb_ch, k)])
            ln_stats(P, lambda k: bslice(hb_ch, k, TCH), lambda k: bslice(sqb_ch, k, TCH),
                     row, ps[6], ps[7], ones_c, ones_rf, c_eps,
                     [[(hb_ch, k) for k in range(DB)], [(sqb_ch, k) for k in range(DB)]])
            for m in range(DB):
                ut = u_t[m % 2]
                P.emit("v", lambda E, m=m, ut=ut: E.tensor_mul(ut[:, :], bslice(h_f, m, TCH), ps[6][:, :]),
                       rd=[(h_f, m), (ps[6], "*")], wr=[(ut, "*")])
                P.emit("v", lambda E, m=m, ut=ut: E.tensor_sub(ut[:, :], ut[:, :], ps[7][:, :]),
                       rd=[(ut, "*"), (ps[7], "*")], wr=[(ut, "*")])
                P.emit("v", lambda E, m=m, ut=ut: E.scalar_tensor_tensor(
                    ut[:, :], ut[:, :], g2c[:, m:m + 1], b2c[:, m:m + 1].to_broadcast([128, TCH]),
                    ALU.mult, ALU.add), rd=[(ut, "*"), (g2c, "*"), (b2c, "*")], wr=[(ut, "*")])
                P.emit("v", lambda E, m=m, ut=ut: E.tensor_copy(bslice(t1b, m, TCH), ut[:, :]),
                       rd=[(ut, "*")], wr=[(t1b, m)])
                P.emit("v", lambda E, m=m, ut=ut: E.tensor_sub(
                    t1lo[:, m * TCH:(m + 1) * TCH], ut[:, :], bslice(t1b, m, TCH)),
                    rd=[(ut, "*"), (t1b, m)], wr=[(a_buf, ("lo", m))])
            P.emit("g", lambda E, b=b, lo=lo: E.dma_start(
                dr["t_scr"][b, :, lo:lo + TCH].rearrange("(k p) t -> p k t", p=128),
                t1b[:, :].rearrange("p (k t) -> p k t", k=DB)),
                rd=[(t1b, m) for m in range(DB)], wr=[("t_scr", lo)], dma="to")
            for tb in range(TCH // 128):
                tbg = c * (TCH // 128) + tb

                def lmm(E, tb=tb):
                    for k in range(DB):
                        E.matmul(ps[2][:, 0:8], t1b[:, k * TCH + tb * 128: k * TCH + tb * 128 + 128],
                                 gw_bf[:, k * NE:(k + 1) * NE], start=(k == 0), stop=False)
                    for k in range(DB):
                        E.matmul(ps[2][:, 0:8], t1lo[:, k * TCH + tb * 128: k * TCH + tb * 128 + 128],
                                 gw_bf[:, k * NE:(k + 1) * NE], start=False, stop=False)
                    for k in range(DB):
                        ii = E.matmul(ps[2][:, 0:8], t1b[:, k * TCH + tb * 128: k * TCH + tb * 128 + 128],
                                      gwlo_bf[:, k * NE:(k + 1) * NE], start=False, stop=(k == DB - 1))
                    return ii
                P.emit("t", lmm, rd=[(gw_bf, "*"), (gwlo_bf, "*")] + [(t1b, k) for k in range(DB)] +
                       [(a_buf, ("lo", k)) for k in range(DB)], wr=[(ps[2], "*")])
                cbufs = dict(lg_sb=lg_sb, m8=m8, scr=scr, mrk=mrk, mskA=mskA, mskB=mskB, wcol=wcol)
                emit_comb(P, ps[2][:, 0:8], comb_all[:, tbg * NE:(tbg + 1) * NE], (comb_all, tbg), cbufs)

        for tbg in range(T // 128):
            for ee in range(NE):
                P.emit("t", lambda E, tbg=tbg, ee=ee: E.transpose(
                    ps[2][0:1, 0:128], comb_all[:, tbg * NE + ee: tbg * NE + ee + 1], ident[:, :]),
                    rd=[(comb_all, tbg), (ident, "*")], wr=[(ps[2], "*")])
                P.emit("v", lambda E, tbg=tbg, ee=ee: E.tensor_copy(
                    ctstage[0:1, ee * 128:(ee + 1) * 128], ps[2][0:1, 0:128]),
                    rd=[(ps[2], "*")], wr=[(ctstage, ee)])
            P.emit("g", lambda E, b=b, tbg=tbg: E.dma_start(
                dr["combT_scr"][:, tbg * 128:(tbg + 1) * 128].unsqueeze(0),
                ctstage[0:1, :].rearrange("one (e p) -> one e p", e=NE)),
                rd=[(ctstage, ee) for ee in range(NE)], wr=[("combT_scr", tbg)], dma="cto")

        for half in range(2):
            hlo = half * HALF
            P.emit("y", lambda E, b=b, hlo=hlo: E.dma_start(
                t_half[:, :].rearrange("p (k t) -> p k t", k=DB),
                dr["t_scr"][b, :, hlo:hlo + HALF].rearrange("(k p) t -> p k t", p=128)),
                rd=[("t_scr", "*")], wr=[(t_half, "*")], dma="th")
            for ee in range(NE):
                for cc in range(2):
                    P.emit("y", lambda E, ee=ee, hlo=hlo, cc=cc: E.dma_start(
                        combT_sb[0:1, :], dr["combT_scr"][ee, hlo + cc * TCH:hlo + (cc + 1) * TCH].unsqueeze(0)),
                        rd=[("combT_scr", "*")], wr=[(combT_sb, "*")], dma="cti")
                    P.emit("t", lambda E, cc=cc: E.matmul(
                        ps[4 + cc][:, :], ones_rf[:, :], combT_sb[0:1, :],
                        start=True, stop=True),
                        rd=[(ones_rf, "*"), (combT_sb, "*")], wr=[(ps[4 + cc], "*")])
                    P.emit("v", lambda E, cc=cc: E.tensor_copy(crep[:, cc * TCH:(cc + 1) * TCH], ps[4 + cc][:, :]),
                           rd=[(ps[4 + cc], "*")], wr=[(crep, cc)])
                for grp in range(IB // HGRP):
                    wb = wg_buf[grp % 2]
                    ub = wu_buf[grp % 2]
                    P.emit("g", lambda E, ee=ee, grp=grp, wb=wb: E.dma_start(
                        wb[:, :].rearrange("p (k i) -> p k i", k=DB),
                        dr["gpT"][ee, :, grp * HGRP * 128:(grp + 1) * HGRP * 128].rearrange("(k p) i -> p k i", p=128)),
                        wr=[(wb, "*")], dma="wg%d" % (grp % 2))
                    P.emit("g", lambda E, ee=ee, grp=grp, ub=ub: E.dma_start(
                        ub[:, :].rearrange("p (k i) -> p k i", k=DB),
                        dr["upT"][ee, :, grp * HGRP * 128:(grp + 1) * HGRP * 128].rearrange("(k p) i -> p k i", p=128)),
                        wr=[(ub, "*")], dma="wu%d" % (grp % 2))
                    for i2 in range(HGRP):
                        iblk = grp * HGRP + i2
                        for cc in range(2):
                            pg = ps[cc]
                            pu = ps[2 + cc]

                            def gmm(E, wb=wb, i2=i2, cc=cc, pg=pg):
                                for k in range(DB):
                                    ii = E.matmul(pg[:, :],
                                                  wb[:, k * HGRP * 128 + i2 * 128: k * HGRP * 128 + i2 * 128 + 128],
                                                  t_half[:, k * HALF + cc * TCH: k * HALF + (cc + 1) * TCH],
                                                  start=(k == 0), stop=(k == DB - 1))
                                return ii
                            P.emit("t", gmm, rd=[(wb, "*"), (t_half, "*")], wr=[(pg, "*")])

                            def umm(E, ub=ub, i2=i2, cc=cc, pu=pu):
                                for k in range(DB):
                                    ii = E.matmul(pu[:, :],
                                                  ub[:, k * HGRP * 128 + i2 * 128: k * HGRP * 128 + i2 * 128 + 128],
                                                  t_half[:, k * HALF + cc * TCH: k * HALF + (cc + 1) * TCH],
                                                  start=(k == 0), stop=(k == DB - 1))
                                return ii
                            P.emit("t", umm, rd=[(ub, "*"), (t_half, "*")], wr=[(pu, "*")])
                            P.emit("s", lambda E, pg=pg: E.activation(sg_bf[:, :], pg[:, :], AF.Silu),
                                   rd=[(pg, "*")], wr=[(sg_bf, "*")])
                            P.emit("v", lambda E, pu=pu, cc=cc: E.tensor_mul(uu_bf[:, :], pu[:, :], crep[:, cc * TCH:(cc + 1) * TCH]),
                                   rd=[(pu, "*"), (crep, cc)], wr=[(uu_bf, "*")])
                            P.emit("v", lambda E, iblk=iblk, cc=cc: E.tensor_mul(
                                a_buf[:, iblk * HALF + cc * TCH: iblk * HALF + (cc + 1) * TCH], sg_bf[:, :], uu_bf[:, :]),
                                rd=[(sg_bf, "*"), (uu_bf, "*")], wr=[(a_buf, iblk)])
                for mgrp in range(4):
                    for kgrp in range(IB // DGRP):
                        dbuf = wd_buf[kgrp % 2]
                        P.emit("g", lambda E, ee=ee, mgrp=mgrp, kgrp=kgrp, dbuf=dbuf: E.dma_start(
                            dbuf[:, :].rearrange("p (k m) -> p k m", k=DGRP),
                            dr["dpT"][ee, kgrp * DGRP * 128:(kgrp + 1) * DGRP * 128,
                                      mgrp * 256:(mgrp + 1) * 256].rearrange("(k p) m -> p k m", p=128)),
                            wr=[(dbuf, "*")], dma="wd%d" % (kgrp % 2))
                        for m2 in range(2):
                            m = mgrp * 2 + m2
                            for cc in range(2):
                                pd = ps[4 + m2 * 2 + cc]

                                def dmm(E, dbuf=dbuf, m2=m2, cc=cc, pd=pd, kgrp=kgrp):
                                    for kk in range(DGRP):
                                        ii = E.matmul(pd[:, :],
                                                      dbuf[:, kk * 256 + m2 * 128: kk * 256 + m2 * 128 + 128],
                                                      a_buf[:, (kgrp * DGRP + kk) * HALF + cc * TCH:
                                                            (kgrp * DGRP + kk) * HALF + (cc + 1) * TCH],
                                                      start=(kgrp == 0 and kk == 0), stop=(kgrp == IB // DGRP - 1 and kk == DGRP - 1))
                                    return ii
                                P.emit("t", dmm, rd=[(dbuf, "*")] + [(a_buf, kgrp * DGRP + kk) for kk in range(DGRP)],
                                       wr=[(pd, "dn%d" % (m2 * 2 + cc))] if kgrp == 0 else [(pd, "dn%d" % (m2 * 2 + cc))])
                    for m2 in range(2):
                        m = mgrp * 2 + m2
                        for cc in range(2):
                            pd = ps[4 + m2 * 2 + cc]
                            if ee == 0:
                                P.emit("v", lambda E, m=m, cc=cc, pd=pd: E.tensor_copy(
                                    acc[:, m * HALF + cc * TCH: m * HALF + (cc + 1) * TCH], pd[:, :]),
                                    rd=[(pd, "*")], wr=[(acc, (m, cc))])
                            else:
                                P.emit("v", lambda E, m=m, cc=cc, pd=pd: E.tensor_add(
                                    acc[:, m * HALF + cc * TCH: m * HALF + (cc + 1) * TCH],
                                    acc[:, m * HALF + cc * TCH: m * HALF + (cc + 1) * TCH], pd[:, :]),
                                    rd=[(pd, "*"), (acc, (m, cc))], wr=[(acc, (m, cc))])

            for cc in range(2):
                c = half * 2 + cc
                lo = c * TCH
                P.emit("y", lambda E, b=b, lo=lo: E.dma_start(
                    h_f[:, :].rearrange("p (k t) -> p k t", k=DB),
                    dr["h_scr"][b, :, lo:lo + TCH].rearrange("(k p) t -> p k t", p=128)),
                    rd=[("h_scr", "*")], wr=[(h_f, m) for m in range(DB)], dma="hi")
                for m in range(DB):
                    xf = xf_t[m % 2]
                    P.emit("y", lambda E, b=b, lo=lo, m=m, xf=xf: E.dma_start(
                        xf[:, :], dr["x1T"][b, m * 128:(m + 1) * 128, lo:lo + TCH]),
                        wr=[(xf, "*")], dma="xf")
                    P.emit("v", lambda E, m=m, cc=cc, xf=xf: E.tensor_add(
                        bslice(h_f, m, TCH), bslice(h_f, m, TCH), xf[:, :]),
                        rd=[(h_f, m), (xf, "*")], wr=[(h_f, m)])
                    P.emit("v", lambda E, m=m, cc=cc: E.tensor_add(
                        bslice(h_f, m, TCH), bslice(h_f, m, TCH),
                        acc[:, m * HALF + cc * TCH: m * HALF + (cc + 1) * TCH]),
                        rd=[(h_f, m), (acc, (m, cc))], wr=[(h_f, m)])
                    P.emit("s", lambda E, m=m: E.activation(bslice(hb_ch, m, TCH), bslice(h_f, m, TCH), AF.Copy),
                           rd=[(h_f, m)], wr=[(hb_ch, m)])
                    P.emit("s", lambda E, m=m: E.activation(bslice(sqb_ch, m, TCH), bslice(h_f, m, TCH), AF.Square),
                           rd=[(h_f, m)], wr=[(sqb_ch, m)])
                ln_stats(P, lambda k: bslice(hb_ch, k, TCH), lambda k: bslice(sqb_ch, k, TCH),
                         row, ps[2], ps[3], ones_c, ones_rf, c_eps,
                         [[(hb_ch, k) for k in range(DB)], [(sqb_ch, k) for k in range(DB)]])
                for m in range(DB):
                    ut = u_t[m % 2]
                    P.emit("v", lambda E, m=m, ut=ut: E.tensor_mul(ut[:, :], bslice(h_f, m, TCH), ps[2][:, :]),
                           rd=[(h_f, m), (ps[2], "*")], wr=[(ut, "*")])
                    P.emit("v", lambda E, m=m, ut=ut: E.tensor_sub(bslice(h_f, m, TCH), ut[:, :], ps[3][:, :]),
                           rd=[(ut, "*"), (ps[3], "*")], wr=[(h_f, m)])
                    P.emit("v", lambda E, m=m: E.tensor_copy(bslice(x1b_ch, m, TCH), bslice(h_f, m, TCH)),
                           rd=[(h_f, m)], wr=[(x1b_ch, "*")])
                P.emit("y", lambda E, b=b, lo=lo: E.dma_start(
                    dr["x2T"][b, :, lo:lo + TCH].rearrange("(k p) t -> p k t", p=128),
                    h_f[:, :].rearrange("p (k t) -> p k t", k=DB)),
                    rd=[(h_f, m) for m in range(DB)], dma="x2o")
                P.emit("g", lambda E, b=b, lo=lo: E.dma_start(
                    dr["x2bT"][b, :, lo:lo + TCH].rearrange("(k p) t -> p k t", p=128),
                    x1b_ch[:, :].rearrange("p (k t) -> p k t", k=DB)),
                    rd=[(x1b_ch, "*")], dma="x2bo")
        P.barrier()


def _host_prep_b(d):
    h = {}
    h["awT"] = np.ascontiguousarray(np.asarray(d["L0_ffn_attn_w"]).T)
    h["gwT"] = np.ascontiguousarray(np.asarray(d["L0_ffn_gate_w"]).T)
    for nm, key in [("ln1_g", "L0_ffn_ln1_g"), ("ln1_b", "L0_ffn_ln1_b"),
                    ("ln2_g", "L0_ffn_ln2_g"), ("ln2_b", "L0_ffn_ln2_b")]:
        h[nm] = np.ascontiguousarray(np.asarray(d[key]).reshape(DB, 128).T.astype(np.float32))
    h["gpT"] = np.ascontiguousarray(np.asarray(d["L0_ffn_gate_proj"]).transpose(0, 2, 1))
    h["upT"] = np.ascontiguousarray(np.asarray(d["L0_ffn_up_proj"]).transpose(0, 2, 1))
    h["dpT"] = np.ascontiguousarray(np.asarray(d["L0_ffn_down_proj"]).transpose(0, 2, 1))
    h["identity"] = np.eye(128, dtype=np.float32)
    return h


B_INPUTS = [("x1T", None, F32), ("x1bT", None, BF16), ("awT", [D, D], F32), ("gwT", [D, NE], F32),
            ("ln1_g", [128, DB], F32), ("ln1_b", [128, DB], F32), ("ln2_g", [128, DB], F32), ("ln2_b", [128, DB], F32),
            ("gpT", [NE, D, NI], F32), ("upT", [NE, D, NI], F32), ("dpT", [NE, NI, D], F32),
            ("identity", [128, 128], F32)]


def _build_b(nb=NB, dbg=False):
    nc = bass.Bass("TRN2", target_bir_lowering=False, debug=False)
    dr = {}
    for nm, shp, dt in B_INPUTS:
        shp = [nb, D, T] if shp is None else shp
        dr[nm] = nc.dram_tensor(nm, shp, dt, kind="ExternalInput").ap()
    dr["x2T"] = nc.dram_tensor("x2T", [nb, D, T], F32, kind="ExternalOutput").ap()
    dr["x2bT"] = nc.dram_tensor("x2bT", [nb, D, T], BF16, kind="ExternalOutput").ap()
    ik = "ExternalOutput" if dbg else "Internal"
    dr["h_scr"] = nc.dram_tensor("h_scr", [nb, D, T], F32, kind=ik).ap()
    dr["t_scr"] = nc.dram_tensor("t_scr", [nb, D, T], BF16, kind=ik).ap()
    dr["combT_scr"] = nc.dram_tensor("combT_scr", [NE, T], F32, kind=ik).ap()
    with ExitStack() as ctx:
        block = ctx.enter_context(nc.Block())
        prog = Prog(nc, block, ctx)
        build_phase_b(nc, prog, ctx, dr, nb=nb)
    return nc




NH = 16
NKV = 4
DH = 64
QH = T // 2


def build_phase_c(nc, P, ctx, dr, nb=NB):
    e = ctx.enter_context
    c_eps = P.constf(EPS_LN)
    c_epsr = P.constf(EPS_RMS)

    wq_bf = e(nc.sbuf_tensor("wq_bf", [128, DB * D], BF16))
    wk_bf = e(nc.sbuf_tensor("wk_bf", [128, DB * 256], BF16))
    wv_bf = e(nc.sbuf_tensor("wv_bf", [128, DB * 256], BF16))
    wo_bfC = e(nc.sbuf_tensor("wo_bfC", [128, DB * D], BF16))
    wstg = e(nc.sbuf_tensor("wstg", [128, D], F32))
    gcol = e(nc.sbuf_tensor("gcolC", [128, DB], F32))
    rf_sb = e(nc.sbuf_tensor("sb_rf", [128, T], F32))
    oh2 = e(nc.sbuf_tensor("sb_oh2", [128, 2], BF16))
    oh64 = e(nc.sbuf_tensor("sb_oh64", [128, 1], BF16))
    ident_bf = e(nc.sbuf_tensor("sb_identbf", [128, 128], BF16))
    ones_c = e(nc.sbuf_tensor("ones_cC", [128, 1], BF16))
    ones_rf = e(nc.sbuf_tensor("ones_rfC", [1, 128], F32))

    x2b_ch = e(nc.sbuf_tensor("x2b_ch", [128, DB * TCH], BF16))
    sqb_ch = e(nc.sbuf_tensor("sqbC", [128, DB * TCH], BF16))
    xf_t = [e(nc.sbuf_tensor(f"xfC{i}", [128, TCH], F32)) for i in range(2)]
    sf2_t = [e(nc.sbuf_tensor(f"sfC2{i}", [128, TCH], F32)) for i in range(2)]
    u_bf = e(nc.sbuf_tensor("u_bfC", [128, DB * TCH], BF16))
    q_bf = e(nc.sbuf_tensor("q_bfC", [128, DB * T], BF16))
    kaug = [e(nc.sbuf_tensor(f"kaug{i}", [128, T], BF16)) for i in range(NKV)]
    vaug = [e(nc.sbuf_tensor(f"vaug{i}", [128, 16 * (DH + 1)], BF16)) for i in range(NKV)]
    att_h = e(nc.sbuf_tensor("att_h", [128, 16 * TCH], BF16))
    s_bf = e(nc.sbuf_tensor("s_bfC", [128, DB * T], BF16))
    q_scr = [e(nc.sbuf_tensor(f"qscr{i}", [128, TCH], BF16)) for i in range(2)]
    ksq = x2b_ch
    qsq_t = e(nc.sbuf_tensor("qsqC", [128, TCH], BF16))
    inv_sb = e(nc.sbuf_tensor("invC", [128, TCH], F32))
    sf_t = e(nc.sbuf_tensor("sfC", [128, TCH], F32))
    ktmp = e(nc.sbuf_tensor("ktmpC", [128, TCH], BF16))
    r_f = att_h[:, 0:2 * DB * TCH].bitcast(F32)
    rb_ch = u_bf
    u_t0 = e(nc.sbuf_tensor("u_tC", [128, TCH], F32))
    row = [e(nc.sbuf_tensor(f"crow{i}", [1, TCH], F32)) for i in range(4)]
    sca = [e(nc.sbuf_tensor(f"csca{i}", [1, 1], F32)) for i in range(6)]

    ps = [e(nc.psum_tensor(f"psC{i}", [128, TCH], F32)) for i in range(7)]
    pbt = e(nc.psum_tensor("psCt", [128, 1024], BF16))

    P.emit("y", lambda E: E.dma_start(gcol[:, :], dr["gC"][:, :]), wr=[(gcol, "*")], dma="cC")
    P.emit("y", lambda E: E.dma_start(rf_sb[:, :], dr["rf"][:, :]), wr=[(rf_sb, "*")], dma="cC")
    P.emit("y", lambda E: E.dma_start(oh2[:, :], dr["oh2"][:, :]), wr=[(oh2, "*")], dma="cC")
    P.emit("y", lambda E: E.dma_start(oh64[:, :], dr["oh64"][:, :]), wr=[(oh64, "*")], dma="cC")
    P.emit("y", lambda E: E.dma_start(ident_bf[:, :], dr["identity_bf"][:, :]), wr=[(ident_bf, "*")], dma="cC")
    for nm, buf, w in [("w_q", wq_bf, D), ("w_k", wk_bf, 256), ("w_v", wv_bf, 256)]:
        for k in range(DB):
            P.emit("y", lambda E, nm=nm, k=k, w=w: E.dma_start(
                wstg[:, 0:w], dr[nm][k * 128:(k + 1) * 128, :]), wr=[(wstg, "*")], dma="cC")
            P.emit("v", lambda E, buf=buf, k=k, w=w: E.tensor_scalar_mul(
                buf[:, k * w:(k + 1) * w], wstg[:, 0:w], gcol[:, k:k + 1]),
                rd=[(wstg, "*"), (gcol, "*")], wr=[(buf, k)])
    P.emit("g", lambda E: E.dma_start(wo_bfC[:, :].rearrange("p (k c) -> p k c", k=DB),
                                      dr["w_o"].rearrange("(k p) c -> p k c", p=128)), wr=[(wo_bfC, "*")], dma="wC")
    P.emit("v", lambda E: E.memset(ones_c[:, :], 1.0), wr=[(ones_c, "*")])
    P.emit("v", lambda E: E.memset(ones_rf[:, :], 1.0), wr=[(ones_rf, "*")])
    for kv in range(NKV):
        P.emit("v", lambda E, kv=kv: E.memset(kaug[kv][64:96, :], 0.0), wr=[(kaug[kv], "z")])
        P.emit("v", lambda E, kv=kv: E.memset(kaug[kv][64:65, :], 1.0), wr=[(kaug[kv], "z")])
        P.emit("v", lambda E, kv=kv: E.memset(vaug[kv][:, :], 1.0), wr=[(vaug[kv], "z")])
    P.barrier()

    for b in range(nb):
        for c in range(NCH):
            lo = c * TCH
            P.emit("y", lambda E, b=b, lo=lo: E.dma_start(
                x2b_ch[:, :].rearrange("p (k t) -> p k t", k=DB),
                dr["x2bT"][b, :, lo:lo + TCH].rearrange("(k p) t -> p k t", p=128)),
                wr=[(x2b_ch, "*")], dma="xbC")
            for k in range(DB):
                P.emit("s", lambda E, k=k: E.activation(bslice(sqb_ch, k, TCH), bslice(x2b_ch, k, TCH), AF.Square),
                       rd=[(x2b_ch, "*")], wr=[(sqb_ch, k)])

            def smm(E):
                for k in range(DB):
                    ii = E.matmul(ps[6][0:1, :], ones_c[:, :], bslice(sqb_ch, k, TCH), start=(k == 0), stop=(k == DB - 1))
                return ii
            P.emit("t", smm, rd=[(ones_c, "*")] + [(sqb_ch, k) for k in range(DB)], wr=[(ps[6], "*")])
            P.emit("v", lambda E: E.tensor_scalar_mul(row[0][:, :], ps[6][0:1, :], 1.0 / D), rd=[(ps[6], "*")], wr=[(row[0], "*")])
            P.emit("s", lambda E: E.activation(row[1][0:1, :], row[0][0:1, :], AF.Ln, bias=c_epsr[0:1, 0:1]),
                   rd=[(row[0], "*"), (c_epsr, "*")], wr=[(row[1], "*")])
            P.emit("s", lambda E: E.activation(row[2][0:1, :], row[1][0:1, :], AF.Exp, scale=-0.5),
                   rd=[(row[1], "*")], wr=[(row[2], "*")])
            P.emit("t", lambda E: E.matmul(ps[6][:, :], ones_rf[:, :], row[2][:, :], start=True, stop=True),
                   rd=[(ones_rf, "*"), (row[2], "*")], wr=[(ps[6], "*")])
            for m in range(DB):
                xf = xf_t[m % 2]
                P.emit("y", lambda E, b=b, lo=lo, m=m, xf=xf: E.dma_start(
                    xf[:, :], dr["x2T"][b, m * 128:(m + 1) * 128, lo:lo + TCH]),
                    wr=[(xf, "*")], dma="xfC")
                P.emit("v", lambda E, m=m, xf=xf: E.tensor_mul(bslice(u_bf, m, TCH), xf[:, :], ps[6][:, :]),
                       rd=[(xf, "*"), (ps[6], "*")], wr=[(u_bf, m)])
            for m in range(DB):
                pz = ps[m % 2]

                def qmm(E, m=m, pz=pz):
                    for k in range(DB):
                        ii = E.matmul(pz[:, :], wq_bf[:, k * D + m * 128: k * D + m * 128 + 128],
                                      bslice(u_bf, k, TCH), start=(k == 0), stop=(k == DB - 1))
                    return ii
                P.emit("t", qmm, rd=[(wq_bf, k) for k in range(DB)] + [(u_bf, k) for k in range(DB)], wr=[(pz, "*")])
                P.emit("v", lambda E, m=m, pz=pz, lo=lo: E.tensor_mul(
                    q_bf[:, m * T + lo: m * T + lo + TCH], pz[:, :], rf_sb[:, lo:lo + TCH]),
                    rd=[(pz, "*"), (rf_sb, "*")], wr=[(q_bf, (m, lo))])
            for m in range(2):
                pz = ps[2 + m]

                def kmm(E, m=m, pz=pz):
                    for k in range(DB):
                        ii = E.matmul(pz[:, :], wk_bf[:, k * 256 + m * 128: k * 256 + m * 128 + 128],
                                      bslice(u_bf, k, TCH), start=(k == 0), stop=(k == DB - 1))
                    return ii
                P.emit("t", kmm, rd=[(wk_bf, k) for k in range(DB)] + [(u_bf, k) for k in range(DB)], wr=[(pz, "*")])
                P.emit("v", lambda E, m=m, pz=pz, lo=lo: E.tensor_mul(ktmp[:, :], pz[:, :], rf_sb[:, lo:lo + TCH]),
                       rd=[(pz, "*"), (rf_sb, "*")], wr=[(ktmp, "*")])
                P.emit("v", lambda E, m=m, lo=lo: E.tensor_copy(kaug[2 * m][0:64, lo:lo + TCH], ktmp[0:64, :]),
                       rd=[(ktmp, "*")], wr=[(kaug[2 * m], lo)])
                P.emit("v", lambda E, m=m, lo=lo: E.tensor_copy(kaug[2 * m + 1][0:64, lo:lo + TCH], ktmp[64:128, :]),
                       rd=[(ktmp, "*")], wr=[(kaug[2 * m + 1], lo)])
            for m in range(2):
                pz = ps[4 + m]

                def vmm(E, m=m, pz=pz):
                    for k in range(DB):
                        ii = E.matmul(pz[:, :], wv_bf[:, k * 256 + m * 128: k * 256 + m * 128 + 128],
                                      bslice(u_bf, k, TCH), start=(k == 0), stop=(k == DB - 1))
                    return ii
                P.emit("t", vmm, rd=[(wv_bf, k) for k in range(DB)] + [(u_bf, k) for k in range(DB)], wr=[(pz, "*")])
                P.emit("v", lambda E, pz=pz: E.tensor_copy(ktmp[:, :], pz[:, :]), rd=[(pz, "*")], wr=[(ktmp, "*")])
                for ts_ in range(TCH // 128):
                    blk = (lo // 128) + ts_
                    P.emit("t", lambda E, ts_=ts_: E.transpose(
                        pbt[:, 0:128], ktmp[:, ts_ * 128:(ts_ + 1) * 128], ident_bf[:, :]),
                        rd=[(ktmp, "*"), (ident_bf, "*")], wr=[(pbt, "*")])
                    for h2 in range(2):
                        kv = 2 * m + h2
                        P.emit("v", lambda E, kv=kv, blk=blk, h2=h2: E.tensor_copy(
                            vaug[kv][:, blk * (DH + 1): blk * (DH + 1) + DH],
                            pbt[:, h2 * 64:(h2 + 1) * 64]),
                            rd=[(pbt, "*")], wr=[(vaug[kv], blk)])

        P.emit("v", lambda E: E.memset(ksq[64:128, 0:T], 0.0), wr=[(ksq, "*")])
        for kv in range(NKV):
            P.emit("s", lambda E, kv=kv: E.activation(ksq[0:64, 0:T], kaug[kv][0:64, :], AF.Square),
                   rd=[(kaug[kv], lo) for lo in range(0, T, TCH)] + [(ksq, "*")], wr=[(ksq, "sq")])
            for cq in range(NCH):
                P.emit("t", lambda E, cq=cq: E.matmul(ps[6][0:1, :], oh64[:, :], ksq[:, cq * TCH:(cq + 1) * TCH],
                                                      start=True, stop=True),
                       rd=[(oh64, "*"), (ksq, "sq")], wr=[(ps[6], "*")])
                P.emit("v", lambda E: E.reduce_max(sca[4][:, :], ps[6][0:1, :], mybir.AxisListType.X),
                       rd=[(ps[6], "*")], wr=[(sca[4], "*")])
                if cq == 0:
                    P.emit("v", lambda E, kv=kv: E.tensor_copy(sca[kv][:, :], sca[4][:, :]),
                           rd=[(sca[4], "*")], wr=[(sca[kv], "*")])
                else:
                    P.emit("v", lambda E, kv=kv: E.tensor_max(sca[kv][:, :], sca[kv][:, :], sca[4][:, :]),
                           rd=[(sca[4], "*"), (sca[kv], "*")], wr=[(sca[kv], "*")])
        P.emit("v", lambda E: E.tensor_max(sca[0][:, :], sca[0][:, :], sca[1][:, :]), rd=[(sca[0], "*"), (sca[1], "*")], wr=[(sca[0], "*")])
        P.emit("v", lambda E: E.tensor_max(sca[2][:, :], sca[2][:, :], sca[3][:, :]), rd=[(sca[2], "*"), (sca[3], "*")], wr=[(sca[2], "*")])
        P.emit("v", lambda E: E.tensor_max(sca[0][:, :], sca[0][:, :], sca[2][:, :]), rd=[(sca[0], "*"), (sca[2], "*")], wr=[(sca[0], "*")])
        P.emit("s", lambda E: E.activation(sca[5][:, :], sca[0][:, :], AF.Ln), rd=[(sca[0], "*")], wr=[(sca[5], "*")])
        P.emit("s", lambda E: E.activation(sca[5][:, :], sca[5][:, :], AF.Exp, scale=0.5), rd=[(sca[5], "*")], wr=[(sca[5], "*")])

        for h in range(NH):
            kv = h % NKV
            mq = h // 2
            ho = (h % 2) * 64
            for qh in range(NCH):
                qlo = qh * TCH
                qs = q_scr[qh % 2]
                P.emit("v", lambda E, mq=mq, ho=ho, qlo=qlo, qs=qs: E.tensor_copy(
                    qs[0:64, :], q_bf[ho:ho + 64, mq * T + qlo: mq * T + qlo + TCH]),
                    rd=[(q_bf, (mq, qlo))], wr=[(qs, "*")])
                P.emit("s", lambda E, mq=mq, qlo=qlo: E.activation(
                    qsq_t[:, :], q_bf[:, mq * T + qlo: mq * T + qlo + TCH], AF.Square),
                    rd=[(q_bf, (mq, qlo))], wr=[(qsq_t, "*")])
                P.emit("t", lambda E, ho=ho: E.matmul(
                    ps[6][0:1, :], oh2[:, ho // 64: ho // 64 + 1], qsq_t[:, :],
                    start=True, stop=True), rd=[(oh2, "*"), (qsq_t, "*")], wr=[(ps[6], "*")])
                P.emit("s", lambda E: E.activation(row[0][0:1, :], ps[6][0:1, :], AF.Ln),
                       rd=[(ps[6], "*")], wr=[(row[0], "*")])
                P.emit("s", lambda E: E.activation(row[3][0:1, :], row[0][0:1, :], AF.Exp, scale=0.5),
                       rd=[(row[0], "*")], wr=[(row[3], "*")])
                P.emit("v", lambda E, qs=qs: E.tensor_scalar(
                    qs[64:65, :], row[3][0:1, :], sca[5][0:1, 0:1], -1.05,
                    ALU.mult, ALU.mult), rd=[(row[3], "*"), (sca[5], "*")], wr=[(qs, "*")])
                for kblk in range(16):
                    pz = ps[kblk % 2]
                    P.emit("t", lambda E, kv=kv, kblk=kblk, qs=qs, pz=pz: E.matmul(
                        pz[:, :], kaug[kv][0:65, kblk * 128:(kblk + 1) * 128],
                        qs[0:65, :], start=True, stop=True),
                        rd=[(kaug[kv], (kblk * 128) // TCH * TCH), (kaug[kv], "z"), (qs, "*")], wr=[(pz, "*")])
                    P.emit("s", lambda E, kblk=kblk, pz=pz: E.activation(
                        att_h[:, kblk * TCH:(kblk + 1) * TCH], pz[:, :],
                        AF.Exp, scale=1.0 / 32), rd=[(pz, "*")], wr=[(att_h, kblk)])
                pa = ps[2 + qh % 2]

                def amm(E, kv=kv, pa=pa):
                    for kblk in range(16):
                        ii = E.matmul(pa[0:65, :], vaug[kv][:, kblk * (DH + 1): (kblk + 1) * (DH + 1)],
                                      att_h[:, kblk * TCH:(kblk + 1) * TCH],
                                      start=(kblk == 0), stop=(kblk == 15))
                    return ii
                P.emit("t", amm, rd=[(vaug[kv], "*")] + [(att_h, kblk) for kblk in range(16)], wr=[(pa, "*")])
                P.emit("s", lambda E, pa=pa: E.activation(row[1][0:1, :], pa[64:65, :], AF.Ln),
                       rd=[(pa, "*")], wr=[(row[1], "*")])
                P.emit("s", lambda E: E.activation(row[2][0:1, :], row[1][0:1, :], AF.Exp, scale=-1.0),
                       rd=[(row[1], "*")], wr=[(row[2], "*")])
                P.emit("t", lambda E: E.matmul(ps[4][0:64, :], ones_rf[0:1, 0:64], row[2][:, :], start=True, stop=True),
                       rd=[(ones_rf, "*"), (row[2], "*")], wr=[(ps[4], "*")])
                P.emit("v", lambda E: E.tensor_copy(inv_sb[0:64, :], ps[4][0:64, :]), rd=[(ps[4], "*")], wr=[(inv_sb, "*")])
                P.emit("v", lambda E, mq=mq, ho=ho, qlo=qlo, pa=pa: E.tensor_mul(
                    s_bf[ho:ho + 64, mq * T + qlo: mq * T + qlo + TCH],
                    pa[0:64, :], inv_sb[0:64, :]),
                    rd=[(pa, "*"), (inv_sb, "*")], wr=[(s_bf, (mq, ho, qlo))])
                P.emit("v", lambda E, pa=pa: E.tensor_mul(sf_t[0:64, :], pa[0:64, :], inv_sb[0:64, :]),
                       rd=[(pa, "*"), (inv_sb, "*")], wr=[(sf_t, "*")])
                P.emit("y", lambda E, b=b, h=h, qlo=qlo: E.dma_start(
                    dr["s_scr"][b, h * 64:(h + 1) * 64, qlo: qlo + TCH], sf_t[0:64, :]),
                    rd=[(sf_t, "*")], wr=[("s_scr", (h, qlo))], dma="so")

        for c in range(NCH):
            lo = c * TCH
            for m in range(DB):
                pz = ps[m % 2]
                xf = xf_t[m % 2]
                sf2 = sf2_t[m % 2]

                def omm(E, m=m, pz=pz, lo=lo):
                    for k in range(DB):
                        ii = E.matmul(pz[:, :], wo_bfC[:, k * D + m * 128: k * D + m * 128 + 128],
                                      s_bf[:, k * T + lo: k * T + lo + TCH], start=(k == 0), stop=(k == DB - 1))
                    return ii
                skeys = [(s_bf, (k, hh, qq)) for k in range(DB) for hh in (0, 64) for qq in range(0, T, TCH)]
                claim = [(att_h, "*")] if (c == 0 and m == 0) else []
                P.emit("t", omm, rd=[(wo_bfC, "*")] + skeys, wr=[(pz, "*")] + claim)
                P.emit("y", lambda E, b=b, lo=lo, m=m, xf=xf: E.dma_start(
                    xf[:, :], dr["x2T"][b, m * 128:(m + 1) * 128, lo:lo + TCH]),
                    wr=[(xf, "*")], dma="xfC")
                P.emit("y", lambda E, b=b, lo=lo, m=m, sf2=sf2: E.dma_start(
                    sf2[:, :], dr["s_scr"][b, m * 128:(m + 1) * 128, lo:lo + TCH]),
                    rd=[("s_scr", "*")], wr=[(sf2, "*")], dma="si")
                P.emit("v", lambda E, m=m, pz=pz, xf=xf: E.tensor_add(bslice(r_f, m, TCH), pz[:, :], xf[:, :]),
                       rd=[(pz, "*"), (xf, "*")], wr=[(r_f, m)])
                P.emit("v", lambda E, m=m, sf2=sf2: E.tensor_add(bslice(r_f, m, TCH), bslice(r_f, m, TCH), sf2[:, :]),
                       rd=[(r_f, m), (sf2, "*")], wr=[(r_f, m)])
                P.emit("s", lambda E, m=m: E.activation(bslice(rb_ch, m, TCH), bslice(r_f, m, TCH), AF.Copy),
                       rd=[(r_f, m)], wr=[(rb_ch, m)])
                P.emit("s", lambda E, m=m: E.activation(bslice(sqb_ch, m, TCH), bslice(r_f, m, TCH), AF.Square),
                       rd=[(r_f, m)], wr=[(sqb_ch, m)])
            ln_stats(P, lambda k: bslice(rb_ch, k, TCH), lambda k: bslice(sqb_ch, k, TCH),
                     row, ps[5], ps[6], ones_c, ones_rf, c_eps,
                     [[(rb_ch, k) for k in range(DB)], [(sqb_ch, k) for k in range(DB)]])
            for m in range(DB):
                P.emit("v", lambda E, m=m: E.tensor_mul(u_t0[:, :], bslice(r_f, m, TCH), ps[5][:, :]),
                       rd=[(r_f, m), (ps[5], "*")], wr=[(u_t0, "*")])
                P.emit("v", lambda E, m=m: E.tensor_sub(bslice(r_f, m, TCH), u_t0[:, :], ps[6][:, :]),
                       rd=[(u_t0, "*"), (ps[6], "*")], wr=[(r_f, m)])
                P.emit("v", lambda E, m=m: E.tensor_copy(bslice(rb_ch, m, TCH), bslice(r_f, m, TCH)),
                       rd=[(r_f, m)], wr=[(rb_ch, m)])
            P.emit("y", lambda E, b=b, lo=lo: E.dma_start(
                dr["x3T"][b, :, lo:lo + TCH].rearrange("(k p) t -> p k t", p=128),
                r_f[:, :].rearrange("p (k t) -> p k t", k=DB)),
                rd=[(r_f, m) for m in range(DB)], dma="x3o")
            P.emit("g", lambda E, b=b, lo=lo: E.dma_start(
                dr["x3bT"][b, :, lo:lo + TCH].rearrange("(k p) t -> p k t", p=128),
                rb_ch[:, :].rearrange("p (k t) -> p k t", k=DB)),
                rd=[(rb_ch, m) for m in range(DB)], dma="x3bo")
        P.barrier()


def _host_prep_c(d):
    h = {}
    h["w_q"] = np.ascontiguousarray(d["L1_attn_w_q"])
    h["w_k"] = np.ascontiguousarray(d["L1_attn_w_k"])
    h["w_v"] = np.ascontiguousarray(d["L1_attn_w_v"])
    h["w_o"] = np.ascontiguousarray(d["L1_attn_w_o"])
    h["gC"] = np.ascontiguousarray(np.asarray(d["L1_attn_g"]).reshape(DB, 128).T.astype(np.float32))
    i = np.arange(DH)
    theta = np.power(np.float32(10000.0), (-2.0 * (i // 2) / DH).astype(np.float32))
    pos = np.arange(T, dtype=np.float32)
    ang = pos[:, None] * theta[None, :]
    sign = np.where(np.arange(T) % 2 == 1, -1.0, 1.0).astype(np.float32)
    rf = np.cos(ang) + sign[:, None] * np.sin(ang)
    h["rf"] = np.ascontiguousarray(np.tile(rf.T.astype(np.float32), (2, 1)))
    oh2 = np.zeros((128, 2), np.float32)
    oh2[0:64, 0] = 1.0
    oh2[64:128, 1] = 1.0
    h["oh2"] = oh2.astype(BF16NP)
    oh64 = np.zeros((128, 1), np.float32)
    oh64[0:64, 0] = 1.0
    h["oh64"] = oh64.astype(BF16NP)
    h["identity_bf"] = np.eye(128, dtype=np.float32).astype(BF16NP)
    return h


C_INPUTS = [("x2T", None, F32), ("x2bT", None, BF16), ("w_q", [D, D], F32), ("w_k", [D, 256], F32),
            ("w_v", [D, 256], F32), ("w_o", [D, D], F32), ("gC", [128, DB], F32), ("rf", [128, T], F32),
            ("oh2", [128, 2], BF16), ("oh64", [128, 1], BF16), ("identity_bf", [128, 128], BF16)]


def _build_c(nb=NB, dbg=False):
    nc = bass.Bass("TRN2", target_bir_lowering=False, debug=False)
    dr = {}
    for nm, shp, dt in C_INPUTS:
        shp = [nb, D, T] if shp is None else shp
        dr[nm] = nc.dram_tensor(nm, shp, dt, kind="ExternalInput").ap()
    dr["x3T"] = nc.dram_tensor("x3T", [nb, D, T], F32, kind="ExternalOutput").ap()
    dr["x3bT"] = nc.dram_tensor("x3bT", [nb, D, T], BF16, kind="ExternalOutput").ap()
    dr["s_scr"] = nc.dram_tensor("s_scr", [nb, D, T], F32, kind="ExternalOutput" if dbg else "Internal").ap()
    with ExitStack() as ctx:
        block = ctx.enter_context(nc.Block())
        prog = Prog(nc, block, ctx)
        build_phase_c(nc, prog, ctx, dr, nb=nb)
    return nc


RD = 256
RB = RD // 128


def build_phase_d(nc, P, ctx, dr, nb=NB):
    e = ctx.enter_context
    c_eps = P.constf(EPS_LN)

    wu_bf = e(nc.sbuf_tensor("wuD", [128, NE * DB * RD], BF16))
    wv_bf = e(nc.sbuf_tensor("wvD", [128, NE * DB * RD], BF16))
    wo_bf = e(nc.sbuf_tensor("woD", [128, NE * RB * D], BF16))
    gw_bf = e(nc.sbuf_tensor("sb_gwD", [128, DB * NE], BF16))
    ident_bf = e(nc.sbuf_tensor("identD", [128, 128], BF16))
    ones_c = e(nc.sbuf_tensor("ones_cD", [128, 1], BF16))
    ones_rf = e(nc.sbuf_tensor("ones_rfD", [1, 128], F32))

    x3b_ch = e(nc.sbuf_tensor("x3bD", [128, DB * TCH], BF16))
    sqb_ch = e(nc.sbuf_tensor("sqbD", [128, DB * TCH], BF16))
    xf_t0 = e(nc.sbuf_tensor("xfD0", [128, TCH], F32))
    xf_t = [xf_t0, xf_t0]
    comb_all = e(nc.sbuf_tensor("combD", [128, (T // 128) * NE], F32))
    sT_all = e(nc.sbuf_tensor("sTD", [128, NE * RB * T], BF16))
    su_bf = e(nc.sbuf_tensor("suD", [128, RD], BF16))
    sv_bf = e(nc.sbuf_tensor("svD", [128, RD], BF16))
    r_f = e(nc.sbuf_tensor("r_fD", [128, DB * TCH], F32))
    rb_ch = x3b_ch
    u_t0 = e(nc.sbuf_tensor("u_tD", [128, TCH], F32))
    lg_sb = e(nc.sbuf_tensor("lgD", [128, 8], F32))
    m8 = e(nc.sbuf_tensor("m8D", [128, 8], F32))
    scr = e(nc.sbuf_tensor("scrD", [128, 8], F32))
    mrk = e(nc.sbuf_tensor("mrkD", [128, 8], F32))
    mskA = e(nc.sbuf_tensor("mskAD", [128, 8], F32))
    mskB = e(nc.sbuf_tensor("mskBD", [128, 8], F32))
    wcol = [e(nc.sbuf_tensor(f"wcolD{i}", [128, 1], F32)) for i in range(3)]
    row = [e(nc.sbuf_tensor(f"drow{i}", [1, TCH], F32)) for i in range(4)]

    ps = [e(nc.psum_tensor(f"psD{i}", [128, TCH], F32)) for i in range(7)]
    pbt = e(nc.psum_tensor("psDt", [128, 1024], BF16))

    P.emit("g", lambda E: E.dma_start(wu_bf[:, :].rearrange("p (e k r) -> p (e k) r", e=NE, k=DB),
                                      dr["Wu"].rearrange("e (k p) r -> p (e k) r", p=128)), wr=[(wu_bf, "*")], dma="wD")
    P.emit("g", lambda E: E.dma_start(wv_bf[:, :].rearrange("p (e k r) -> p (e k) r", e=NE, k=DB),
                                      dr["Wv"].rearrange("e (k p) r -> p (e k) r", p=128)), wr=[(wv_bf, "*")], dma="wD")
    P.emit("g", lambda E: E.dma_start(wo_bf[:, :].rearrange("p (e k c) -> p (e k) c", e=NE, k=RB),
                                      dr["Wo"].rearrange("e (k p) c -> p (e k) c", p=128)), wr=[(wo_bf, "*")], dma="wD")
    P.emit("g", lambda E: E.dma_start(gw_bf[:, :].rearrange("p (k c) -> p k c", k=DB),
                                      dr["gwD"].rearrange("(k p) c -> p k c", p=128)), wr=[(gw_bf, "*")], dma="wD")
    P.emit("y", lambda E: E.dma_start(ident_bf[:, :], dr["identity_bf"][:, :]), wr=[(ident_bf, "*")], dma="cD")
    P.emit("v", lambda E: E.memset(ones_c[:, :], 1.0), wr=[(ones_c, "*")])
    P.emit("v", lambda E: E.memset(ones_rf[:, :], 1.0), wr=[(ones_rf, "*")])
    P.barrier()

    cbufs = dict(lg_sb=lg_sb, m8=m8, scr=scr, mrk=mrk, mskA=mskA, mskB=mskB, wcol=wcol)
    for b in range(nb):
        for c in range(NCH):
            lo = c * TCH
            P.emit("y", lambda E, b=b, lo=lo: E.dma_start(
                x3b_ch[:, :].rearrange("p (k t) -> p k t", k=DB),
                dr["x3bT"][b, :, lo:lo + TCH].rearrange("(k p) t -> p k t", p=128)),
                wr=[(x3b_ch, "*")], dma="xbD")
            for tb in range(TCH // 128):
                tbg = c * (TCH // 128) + tb

                def lmm(E, tb=tb):
                    for k in range(DB):
                        ii = E.matmul(ps[2][:, 0:8], x3b_ch[:, k * TCH + tb * 128: k * TCH + tb * 128 + 128],
                                      gw_bf[:, k * NE:(k + 1) * NE], start=(k == 0), stop=(k == DB - 1))
                    return ii
                P.emit("t", lmm, rd=[(gw_bf, "*"), (x3b_ch, "*")], wr=[(ps[2], "*")])
                emit_comb(P, ps[2][:, 0:8], comb_all[:, tbg * NE:(tbg + 1) * NE], (comb_all, tbg), cbufs)
                for ee in range(NE):
                    pu = ps[0]
                    pv = ps[1]

                    def ummD(E, ee=ee, tb=tb, pu=pu):
                        for k in range(DB):
                            ii = E.matmul(pu[:, 0:RD], x3b_ch[:, k * TCH + tb * 128: k * TCH + tb * 128 + 128],
                                          wu_bf[:, (ee * DB + k) * RD:(ee * DB + k + 1) * RD],
                                          start=(k == 0), stop=(k == DB - 1))
                        return ii
                    P.emit("t", ummD, rd=[(wu_bf, "*"), (x3b_ch, "*")], wr=[(pu, "*")])

                    def vmmD(E, ee=ee, tb=tb, pv=pv):
                        for k in range(DB):
                            ii = E.matmul(pv[:, 0:RD], x3b_ch[:, k * TCH + tb * 128: k * TCH + tb * 128 + 128],
                                          wv_bf[:, (ee * DB + k) * RD:(ee * DB + k + 1) * RD],
                                          start=(k == 0), stop=(k == DB - 1))
                        return ii
                    P.emit("t", vmmD, rd=[(wv_bf, "*"), (x3b_ch, "*")], wr=[(pv, "*")])
                    P.emit("s", lambda E, pu=pu: E.activation(su_bf[:, :], pu[:, 0:RD], AF.Silu),
                           rd=[(pu, "*")], wr=[(su_bf, "*")])
                    P.emit("v", lambda E, pv=pv: E.tensor_mul(sv_bf[:, :], pv[:, 0:RD], su_bf[:, :]),
                           rd=[(pv, "*"), (su_bf, "*")], wr=[(sv_bf, "*")])
                    P.emit("v", lambda E, ee=ee, tbg=tbg: E.tensor_scalar_mul(
                        sv_bf[:, :], sv_bf[:, :], comb_all[:, tbg * NE + ee: tbg * NE + ee + 1]),
                        rd=[(sv_bf, "*"), (comb_all, tbg)], wr=[(sv_bf, "*")])
                    for rb_ in range(RB):
                        P.emit("t", lambda E, rb_=rb_: E.transpose(
                            pbt[:, 0:128], sv_bf[:, rb_ * 128:(rb_ + 1) * 128], ident_bf[:, :]),
                            rd=[(sv_bf, "*"), (ident_bf, "*")], wr=[(pbt, "*")])
                        P.emit("v", lambda E, ee=ee, rb_=rb_, tbg=tbg: E.tensor_copy(
                            sT_all[:, (ee * RB + rb_) * T + tbg * 128:(ee * RB + rb_) * T + (tbg + 1) * 128],
                            pbt[:, 0:128]),
                            rd=[(pbt, "*")], wr=[(sT_all, (ee, rb_, tbg))])

        for c in range(NCH):
            lo = c * TCH
            for m in range(DB):
                pz = ps[m % 2]
                xf = xf_t[m % 2]

                def ymmD(E, m=m, pz=pz, lo=lo):
                    n_ = 0
                    for ee in range(NE):
                        for rb_ in range(RB):
                            n_ += 1
                            ii = E.matmul(pz[:, :], wo_bf[:, (ee * RB + rb_) * D + m * 128:(ee * RB + rb_) * D + m * 128 + 128],
                                          sT_all[:, (ee * RB + rb_) * T + lo:(ee * RB + rb_) * T + lo + TCH],
                                          start=(n_ == 1), stop=(n_ == NE * RB))
                    return ii
                stk = [(sT_all, (ee, rb_, tbg)) for ee in range(NE) for rb_ in range(RB)
                       for tbg in range(c * (TCH // 128), (c + 1) * (TCH // 128))]
                P.emit("t", ymmD, rd=[(wo_bf, "*")] + stk, wr=[(pz, "*")])
                P.emit("y", lambda E, b=b, lo=lo, m=m, xf=xf: E.dma_start(
                    xf[:, :], dr["x3T"][b, m * 128:(m + 1) * 128, lo:lo + TCH]),
                    wr=[(xf, "*")], dma="xfD")
                P.emit("v", lambda E, m=m, pz=pz, xf=xf: E.tensor_add(bslice(r_f, m, TCH), pz[:, :], xf[:, :]),
                       rd=[(pz, "*"), (xf, "*")], wr=[(r_f, m)])
                P.emit("s", lambda E, m=m: E.activation(bslice(rb_ch, m, TCH), bslice(r_f, m, TCH), AF.Copy),
                       rd=[(r_f, m)], wr=[(rb_ch, m)])
                P.emit("s", lambda E, m=m: E.activation(bslice(sqb_ch, m, TCH), bslice(r_f, m, TCH), AF.Square),
                       rd=[(r_f, m)], wr=[(sqb_ch, m)])
            ln_stats(P, lambda k: bslice(rb_ch, k, TCH), lambda k: bslice(sqb_ch, k, TCH),
                     row, ps[5], ps[6], ones_c, ones_rf, c_eps,
                     [[(rb_ch, k) for k in range(DB)], [(sqb_ch, k) for k in range(DB)]])
            for m in range(DB):
                P.emit("v", lambda E, m=m: E.tensor_mul(u_t0[:, :], bslice(r_f, m, TCH), ps[5][:, :]),
                       rd=[(r_f, m), (ps[5], "*")], wr=[(u_t0, "*")])
                P.emit("v", lambda E, m=m: E.tensor_sub(bslice(r_f, m, TCH), u_t0[:, :], ps[6][:, :]),
                       rd=[(u_t0, "*"), (ps[6], "*")], wr=[(r_f, m)])
            P.emit("y", lambda E, b=b, lo=lo: E.dma_start(
                dr["x4T"][b, :, lo:lo + TCH].rearrange("(k p) t -> p k t", p=128),
                r_f[:, :].rearrange("p (k t) -> p k t", k=DB)),
                rd=[(r_f, m) for m in range(DB)], dma="x4o")
        P.barrier()


def _host_prep_d(d):
    h = {}
    h["gwD"] = np.ascontiguousarray(np.asarray(d["L1_ffn_gate_w"]))
    h["Wu"] = np.ascontiguousarray(np.asarray(d["L1_ffn_Wu"]))
    h["Wv"] = np.ascontiguousarray(np.asarray(d["L1_ffn_Wv"]))
    h["Wo"] = np.ascontiguousarray(np.asarray(d["L1_ffn_Wo"]))
    h["identity_bf"] = np.eye(128, dtype=np.float32).astype(BF16NP)
    return h


D_INPUTS = [("x3T", None, F32), ("x3bT", None, BF16), ("gwD", [D, NE], F32),
            ("Wu", [NE, D, RD], F32), ("Wv", [NE, D, RD], F32), ("Wo", [NE, RD, D], F32),
            ("identity_bf", [128, 128], BF16)]


def _build_d(nb=NB):
    nc = bass.Bass("TRN2", target_bir_lowering=False, debug=False)
    dr = {}
    for nm, shp, dt in D_INPUTS:
        shp = [nb, D, T] if shp is None else shp
        dr[nm] = nc.dram_tensor(nm, shp, dt, kind="ExternalInput").ap()
    dr["x4T"] = nc.dram_tensor("x4T", [nb, D, T], F32, kind="ExternalOutput").ap()
    with ExitStack() as ctx:
        block = ctx.enter_context(nc.Block())
        prog = Prog(nc, block, ctx)
        build_phase_d(nc, prog, ctx, dr, nb=nb)
    return nc




def _declare_all_inputs(nc, nb):
    dr = {}
    seen = set()
    for nm, shp, dt in A_INPUTS + B_INPUTS + C_INPUTS + D_INPUTS:
        if nm in ("x1T", "x1bT", "x2T", "x2bT", "x3T", "x3bT"):
            continue
        if nm in seen:
            continue
        seen.add(nm)
        shp = [nb, D, T] if nm == "xT" else shp
        dr[nm] = nc.dram_tensor(nm, shp, dt, kind="ExternalInput").ap()
    for nm, dt in [("x1T", F32), ("x1bT", BF16), ("x2T", F32), ("x2bT", BF16),
                   ("x3T", F32), ("x3bT", BF16), ("h_scr", F32), ("s_scr", F32)]:
        dr[nm] = nc.dram_tensor(nm, [nb, D, T], dt, kind="Internal").ap()
    dr["t_scr"] = nc.dram_tensor("t_scr", [nb, D, T], BF16, kind="Internal").ap()
    dr["combT_scr"] = nc.dram_tensor("combT_scr", [NE, T], F32, kind="Internal").ap()
    dr["x4T"] = nc.dram_tensor("x4T", [nb, D, T], F32, kind="ExternalOutput").ap()
    return dr


def _build_fused(nb=NB):
    nc = bass.Bass("TRN2", target_bir_lowering=False, debug=False)
    dr = _declare_all_inputs(nc, nb)
    with ExitStack() as ctx:
        block = ctx.enter_context(nc.Block())
        prog = Prog(nc, block, ctx)
        with ExitStack() as ca:
            build_phase_a(nc, prog, ctx if False else ca, dr, nb=nb)
        prog.barrier()
        with ExitStack() as cb:
            build_phase_b(nc, prog, cb, dr, nb=nb)
        prog.barrier()
        with ExitStack() as cc_:
            build_phase_c(nc, prog, cc_, dr, nb=nb)
        prog.barrier()
        with ExitStack() as cd:
            build_phase_d(nc, prog, cd, dr, nb=nb)
        prog.barrier()
    return nc


_NC_CACHE = {}


def _cached(name, fn):
    if name not in _NC_CACHE:
        _NC_CACHE[name] = fn()
    return _NC_CACHE[name]


def kernel(**inputs):
    d = {k: np.asarray(v) for k, v in inputs.items()}
    m0 = {}
    for hp in (_host_prep_a, _host_prep_b, _host_prep_c, _host_prep_d):
        m0.update(hp(d))
    xT_all = np.ascontiguousarray(d["L0_attn_x"].transpose(0, 2, 1)).astype(np.float32)

    nc = _cached("FUSED", lambda: _build_fused(nb=NB))
    in_maps = []
    for cid in range(NCORES):
        m = dict(m0)
        m["xT"] = xT_all[cid * NB:(cid + 1) * NB]
        in_maps.append(m)
    res = run_bass_kernel_spmd(nc, in_maps, core_ids=list(range(NCORES)))
    out = np.concatenate([np.asarray(r["x4T"]) for r in res.results], axis=0)
    return np.ascontiguousarray(out.transpose(0, 2, 1)).astype(np.float32)
